# version 11
# baseline (speedup 1.0000x reference)
import numpy as np
from contextlib import ExitStack

import concourse.bass as bass
import concourse.mybir as mybir
from concourse.bass_utils import run_bass_kernel_spmd

F32 = mybir.dt.float32
BF16 = mybir.dt.bfloat16
AF = mybir.ActivationFunctionType
ALU = mybir.AluOpType
AX = mybir.AxisListType

DEBUG = False
PH_B = True
PH_C = True
PH_D = True

D = 1024
TQ = 2048
TK = 4096
NT_K = 8
NT_Q = 4
FFD = 4096
HOB = 32


class Sem:
    def __init__(self, handle, name):
        self.h = handle
        self.name = name
        self.n = 0


class Prog:
    def __init__(self):
        self.labels = {}
        self.emit = False
        self.sems = []

    def mksem(self, nc, ctx, name):
        s = Sem(ctx.enter_context(nc.semaphore(name)), name)
        self.sems.append(s)
        return s

    def reset(self):
        for s in self.sems:
            s.n = 0

    def set_label(self, name, s, v):
        if self.emit:
            assert self.labels.get(name) == (s, v), \
                f"label drift {name}: {self.labels.get(name)} vs {(s, v)}"
        else:
            assert name not in self.labels, f"dup label {name}"
            self.labels[name] = (s, v)


class Px:
    def __init__(self, prog, name):
        self.prog = prog
        self.name = name
        self.eng = None

    def op(self, meth, *a, inc=(), label=None, **kw):
        if inc and isinstance(inc[0], Sem):
            inc = [inc]
        inst = None
        if self.prog.emit:
            inst = getattr(self.eng, meth)(*a, **kw)
            for (s, amt) in inc:
                inst.then_inc(s.h, amt)
        for (s, amt) in inc:
            s.n += amt
        if label is not None:
            assert inc, f"label {label} without inc"
            s, _ = inc[0]
            self.prog.set_label(label, s, s.n)
        return inst

    def dma(self, out, in_, sem, label=None, **kw):
        return self.op("dma_start", out, in_, inc=(sem, 16), label=label, **kw)

    def wait(self, label, delta=0):
        if self.prog.emit:
            s, v = self.prog.labels[label]
            self.eng.wait_ge(s.h, v + delta)

    def wait_sem(self, s, v):
        if self.prog.emit:
            self.eng.wait_ge(s.h, v)


ENGS = ["sync", "gpsimd", "tensor", "vector", "scalar"]


def run_blocks(nc, prog, blocks):
    pxs = {e: Px(prog, e) for e in ENGS}
    for emit in (False, True):
        prog.emit = emit
        prog.reset()
        for bname, engs in blocks:
            if not emit:
                for e in ENGS:
                    if e in engs:
                        engs[e](pxs[e])
            else:
                with nc.Block() as blk:
                    for e in ENGS:
                        if e in engs:
                            px, fn = pxs[e], engs[e]

                            def mk(fn=fn, px=px):
                                def _(engine):
                                    px.eng = engine
                                    fn(px)
                                return _
                            getattr(blk, e)(mk())


def build_nc():
    import ml_dtypes
    nc = bass.Bass("TRN2", target_bir_lowering=False,
                   detect_race_conditions=False)

    x_in = nc.dram_tensor("x", [TK, D], F32, kind="ExternalInput").ap()
    wT = {w: nc.dram_tensor(f"w{w}T", [D, D], F32, kind="ExternalInput").ap()
          for w in "qkvp"}
    bvec = {w: nc.dram_tensor(f"b{w}", [8, 128], F32, kind="ExternalInput").ap()
            for w in "qkvp"}
    w1T = nc.dram_tensor("w1T", [D, FFD], F32, kind="ExternalInput").ap()
    w2T = nc.dram_tensor("w2T", [D, FFD], F32, kind="ExternalInput").ap()
    h1T = nc.dram_tensor("h1T", [FFD, FFD], F32, kind="ExternalInput").ap()
    h2T = nc.dram_tensor("h2T", [FFD, FFD], F32, kind="ExternalInput").ap()
    w3T = nc.dram_tensor("w3T", [FFD, D], F32, kind="ExternalInput").ap()
    sc_in = {"s1": nc.dram_tensor("s1", [32, 128], F32, kind="ExternalInput").ap(),
             "s2": nc.dram_tensor("s2", [32, 128], F32, kind="ExternalInput").ap(),
             "sh1": nc.dram_tensor("sh1", [32, 128], F32, kind="ExternalInput").ap(),
             "sh2": nc.dram_tensor("sh2", [32, 128], F32, kind="ExternalInput").ap()}
    s3_in = nc.dram_tensor("s3", [8, 128], F32, kind="ExternalInput").ap()

    out = nc.dram_tensor("out", [TQ, D], F32, kind="ExternalOutput").ap()

    skind = "ExternalOutput" if DEBUG else "Internal"
    q_scr = nc.dram_tensor("q_scr", [D, TQ], BF16, kind=skind).ap()
    k_scr = nc.dram_tensor("k_scr", [D, TK], BF16, kind=skind).ap()
    v_scr = nc.dram_tensor("v_scr", [NT_K, 128, 4160], BF16, kind=skind).ap()
    y_dbg = nc.dram_tensor("y_dbg", [128, 8 * TQ], BF16, kind=skind).ap()
    mh_dbg = nc.dram_tensor("mh_dbg", [1, 48], F32, kind=skind).ap()
    r1_scr = nc.dram_tensor("r1_scr", [8, 128, TQ], F32, kind=skind).ap()
    xffn_scr = nc.dram_tensor("xffn_scr", [8, 128, TQ], BF16, kind=skind).ap()
    w1b = nc.dram_tensor("w1b", [32, 128, 1024], BF16, kind="Internal").ap()
    w2b = nc.dram_tensor("w2b", [32, 128, 1024], BF16, kind="Internal").ap()
    h1b = nc.dram_tensor("h1b", [32, 128, 4096], BF16, kind="Internal").ap()
    h2b = nc.dram_tensor("h2b", [32, 128, 4096], BF16, kind="Internal").ap()
    w3b = nc.dram_tensor("w3b", [8, 128, 4096], BF16, kind="Internal").ap()

    id32_d = nc.inline_tensor(np.eye(128, dtype=np.float32), name="id32c").ap()
    idbf_d = nc.inline_tensor(np.eye(128, dtype=ml_dtypes.bfloat16), name="idbfc").ap()

    prog = Prog()
    ctx = ExitStack()
    nc._kernel_ctx = ctx

    sbt = lambda n, s, dt: ctx.enter_context(nc.sbuf_tensor(n, s, dt))
    psa = lambda n, s, dt: ctx.enter_context(nc.psum_tensor(n, s, dt))

    ABF = sbt("ABF", [128, 66560], BF16)
    AFP = sbt("AFP", [128, 9216], F32)

    id32 = sbt("id32", [128, 128], F32)
    idbf = sbt("idbf", [128, 128], BF16)
    ones_f = sbt("ones_f", [1, 128], F32)
    ones_b = sbt("ones_b", [128, 1], BF16)
    alpha = {w: sbt(f"alpha_{w}", [128, 8], F32) for w in "qkvp"}
    bcol = {w: sbt(f"bcol_{w}", [128, 8], F32) for w in "qkvp"}
    scol = {n: sbt(f"scol_{n}", [128, 32], F32) for n in ("s1", "s2", "sh1", "sh2")}
    s3col = sbt("s3col", [128, 8], F32)
    ln1c = sbt("ln1c", [128, 24], F32)
    MHC = sbt("MHC", [128, 8], F32)
    NRM = sbt("NRM", [1, 48], F32)
    NROWB = sbt("NROWB", [1, 1024], F32)
    RT = sbt("RT", [1, 8], F32)
    R_A = sbt("R_A", [1, 2048], F32)
    R_B = sbt("R_B", [1, 2048], F32)
    SINV = sbt("SINV", [1, 1024], F32)
    YN = sbt("YN", [128, 1024], F32)
    SINVB = sbt("SINVB", [128, 1024], F32)
    YODD = sbt("YODD", [128, 1024], BF16)
    EPSC = sbt("EPSC", [128, 1], F32)

    AB0 = psa("AB0", [128, 1024], F32)
    AB1 = psa("AB1", [128, 1024], F32)
    YPS = psa("YPS", [128, 1024], F32)
    PSR = psa("PSR", [128, 512], F32)
    PST = psa("PST", [128, 1024], BF16)

    s_dg = prog.mksem(nc, ctx, "s_dg")
    s_ds = prog.mksem(nc, ctx, "s_ds")
    s_dx = [prog.mksem(nc, ctx, "s_dx0"), prog.mksem(nc, ctx, "s_dx1")]
    s_dw = prog.mksem(nc, ctx, "s_dw")
    s_db = [prog.mksem(nc, ctx, "s_db0"), prog.mksem(nc, ctx, "s_db1")]
    s_dd = [prog.mksem(nc, ctx, "s_dd0"), prog.mksem(nc, ctx, "s_dd1"),
            prog.mksem(nc, ctx, "s_dd2"), prog.mksem(nc, ctx, "s_dd3")]
    s_dxf = prog.mksem(nc, ctx, "s_dxf")
    s_xr = prog.mksem(nc, ctx, "s_xr")
    s_yod = prog.mksem(nc, ctx, "s_yod")
    s_stk = prog.mksem(nc, ctx, "s_stk")
    s_stq = prog.mksem(nc, ctx, "s_stq")
    s_vst = prog.mksem(nc, ctx, "s_vst")
    s_pc = [prog.mksem(nc, ctx, f"s_pc{i}") for i in range(4)]
    s_po = [prog.mksem(nc, ctx, f"s_po{i}") for i in range(4)]
    s_pe = prog.mksem(nc, ctx, "s_pe")
    s_dv = prog.mksem(nc, ctx, "s_dv")
    s_ac = prog.mksem(nc, ctx, "s_ac")
    s_out = prog.mksem(nc, ctx, "s_out")

    A_WSGN = {"q": 0, "k": 8192, "v": 16384}
    A_WABS = 24576
    A_XBF = [32768, 36864]
    A_XLT = 40960
    A_XLF = [45056, 49152]
    A_KST = 53248
    A_QST = 55296
    A_VFM = 57344
    A_VST = 58368
    A_JUNK = 62528
    A_ABSQ = 63552

    B_QH = [0, 2048]
    B_KP = [4096, 8192]
    B_VH = [12288, 14368, 16448, 18528]
    B_P = [20608, 21632]
    YSLAB = 24576

    C_WPSGN = 0
    C_WPABS = 8192
    C_JUNK = 16384
    C_ATTN = 40960
    C_XFN = 57344
    C_RB = 61440
    C_RBSQ = 61952
    C_XRES1 = 24576
    C_RB8 = 26624
    C_RBSQ8 = 30720

    D_A = 0
    D_B = 16384
    D_JUNK = 20480
    D_HB = [32768, 36864, 40960, 45056]
    D_FO_S = [49152, 53248]
    D_JK8 = 20480

    F_OUT = [0, 8192]
    F_XRES = 0
    F_R1 = 1024
    F_A2 = 5120
    F_B2 = 7168
    F_B2 = 7168
    F_A4 = 5120
    F_B4 = 5632
    F_SCR = 6144

    bf = lambda c0, n: ABF[:, c0:c0 + n]
    fp = lambda c0, n: AFP[:, c0:c0 + n]

    blocks = []

    def p0_sync(p):
        p.dma(id32[:], id32_d, s_ds)
        p.dma(idbf[:], idbf_d, s_ds)
        for i, w in enumerate("qkvp"):
            p.dma(AFP[0:8, 1024 + i * 128:1024 + (i + 1) * 128], bvec[w], s_ds)
        for j, n in enumerate(("s1", "s2", "sh1", "sh2")):
            p.dma(AFP[0:32, 2048 + j * 128:2048 + (j + 1) * 128], sc_in[n], s_ds)
        p.dma(AFP[0:8, 2560:2688], s3_in, s_ds, label="p0_loads")

    def p0_vector(p):
        p.op("memset", ones_f[:], 1.0)
        p.op("memset", ones_b[:], 1.0)
        p.op("memset", NRM[:], 0.0)
        p.op("memset", EPSC[:], 1e-5)
        p.op("memset", bf(A_VST, 4160), 1.0, inc=(s_dv, 1), label="p0_memsets")

    blocks.append(("p0", {"sync": p0_sync, "vector": p0_vector}))

    def p1_tensor(p):
        p.wait("p0_loads")
        p.wait("p0_memsets")
        for i in range(4):
            p.op("transpose", PSR[:, i * 8:(i + 1) * 8],
                 AFP[0:8, 1024 + i * 128:1024 + (i + 1) * 128], id32[0:8, 0:8],
                 inc=((s_pe, 1) if i == 3 else ()), label=("p1_tr1" if i == 3 else None))
        for j in range(4):
            p.op("transpose", AB0[:, j * 32:(j + 1) * 32],
                 AFP[0:32, 2048 + j * 128:2048 + (j + 1) * 128], id32[0:32, 0:32])
        p.op("transpose", AB0[:, 128:136], AFP[0:8, 2560:2688], id32[0:8, 0:8],
             inc=(s_pe, 1), label="p1_tr2")

    def p1_vector(p):
        p.wait("p1_tr1")
        for i, w in enumerate("qkvp"):
            p.op("tensor_copy", bcol[w][:], PSR[:, i * 8:(i + 1) * 8])
        p.wait("p1_tr2")
        for j, n in enumerate(("s1", "s2", "sh1", "sh2")):
            p.op("tensor_copy", scol[n][:], AB0[:, j * 32:(j + 1) * 32])
        p.op("tensor_copy", s3col[:], AB0[:, 128:136])
        p.op("tensor_scalar_mul", bcol["q"][:], bcol["q"][:], 0.125,
             inc=(s_dv, 1), label="p1_cols")

    blocks.append(("p1", {"tensor": p1_tensor, "vector": p1_vector}))

    a_groups = []
    for nt in range(NT_K):
        for w in ("k", "v", "q"):
            if w == "q" and nt >= NT_Q:
                continue
            for ob in range(8):
                a_groups.append((w, nt, ob))
    gidx = {g: i for i, g in enumerate(a_groups)}
    kq_list = [i for i, g in enumerate(a_groups) if g[0] in ("k", "q")]
    kq_pos = {g: j for j, g in enumerate(kq_list)}
    v_list = [i for i, g in enumerate(a_groups) if g[0] == "v"]
    v_pos = {g: j for j, g in enumerate(v_list)}
    w_lists = {w: [i for i, g in enumerate(a_groups) if g[0] == w] for w in "kvq"}
    w_pos = {w: {g: j for j, g in enumerate(w_lists[w])} for w in "kvq"}

    def acc_of(gi):
        t, h = [(AB0, 0), (AB0, 1), (AB1, 0), (AB1, 1)][gi % 4]
        return t[:, h * 512:(h + 1) * 512]

    def a_gpsimd(p):
        for w in "qkv":
            src = wT[w].rearrange("(cc p) o -> p cc o", p=128)
            p.dma(bf(A_WSGN[w], 8192).rearrange("p (cc o) -> p cc o", cc=8),
                  src, s_dw, label=f"a_w{w}")
        for nt in range(NT_K):
            if nt >= 2:
                p.wait(f"a_xuse_{nt - 2}")
                p.wait(f"a_xdve_{nt - 2}")
            src = x_in[nt * 512:(nt + 1) * 512, :].rearrange("(tc p) c -> p tc c", p=128)
            p.dma(bf(A_XBF[nt % 2], 4096).rearrange("p (tc c) -> p tc c", tc=4),
                  src, s_dx[nt % 2], label=f"a_x_{nt}")
        p.wait_sem(s_dw, s_dw.n)
        for sx in s_dx:
            p.wait_sem(sx, sx.n)

    def a_scalar(p):
        prevw = None
        for w in "qkv":
            p.wait("a_wv")
            if prevw is not None:
                p.wait(f"a_acol_{prevw}")
            sl = bf(A_WSGN[w], 8192)
            ab = bf(A_WABS, 8192)
            p.op("activation", ab, sl, AF.Abs, inc=(s_ac, 1), label=f"a_abs_{w}")
            p.op("activation", sl, sl, AF.Sign, inc=(s_ac, 1), label=f"a_sgn_{w}")
            prevw = w
        def emit_sq(n):
            p.wait(f"a_x_{n}")
            for tc in range(4):
                xt = bf(A_XBF[n % 2] + tc * 1024, 1024)
                if n > 0:
                    p.wait(f"a_var_{n - 1}_{tc}")
                p.op("activation", bf(65536, 1024), xt, AF.Square,
                     accum_out=ln1c[:, tc * 6 + 1:tc * 6 + 2], inc=(s_ac, 1),
                     label=f"a_sq_{n}_{tc}")
                if tc == 3:
                    prog.set_label(f"a_xuse_{n}", s_ac, s_ac.n)

        emit_sq(0)
        for nt in range(NT_K):
            if nt + 1 < NT_K:
                emit_sq(nt + 1)
            for tc in range(4):
                r = tc
                p.wait(f"a_var_{nt}_{tc}")
                p.op("activation", ln1c[:, r * 6 + 4:r * 6 + 5],
                     ln1c[:, r * 6 + 3:r * 6 + 4], AF.Sqrt, bias=EPSC[:, 0:1],
                     inc=(s_ac, 1), label=f"a_sqv_{nt}_{tc}")
            for w in ("k", "q"):
                if w == "q" and nt >= NT_Q:
                    continue
                for ob in range(8):
                    gi = gidx[(w, nt, ob)]
                    jw = w_pos[w][gi]
                    p.wait(f"a_ep_{gi}")
                    if jw >= 2:
                        p.wait(f"a_nrow_{w_lists[w][jw - 2]}")
                    st = bf((A_KST if w == "k" else A_QST) + (jw % 4) * 512, 512)
                    dst = bf((A_JUNK if w == "k" else A_ABSQ) + (jw % 2) * 512, 512)
                    p.op("activation", dst, st, AF.Square if w == "k" else AF.Abs,
                         inc=(s_ac, 1), label=f"a_nsq_{gi}")
        p.wait("a_nrm_done")
        p.op("activation", NRM[0:1, 16:32], NRM[0:1, 16:32], AF.Sqrt,
             inc=(s_ac, 1), label="a_mhsq")

    def a_vector(p):
        for w in "qkv":
            p.wait(f"a_alpha_{w}")
            sc = (1.0 / 1024) * (0.125 if w == "q" else 1.0)
            p.op("tensor_scalar_mul", alpha[w][:], PSR[:, 0:8], sc,
                 inc=(s_dv, 1), label=f"a_acol_{w}")
        def ln1_stats(p, n):
            c = ln1c
            p.wait(f"a_x_{n}")
            for tc in range(4):
                xt = bf(A_XBF[n % 2] + tc * 1024, 1024)
                if n > 0:
                    p.wait(f"a_xln_{n - 1}_{tc}")
                p.op("reduce_sum", c[:, tc * 6:tc * 6 + 1], xt, AX.X)
            for tc in range(4):
                p.op("tensor_scalar_mul", c[:, tc * 6:tc * 6 + 1],
                     c[:, tc * 6:tc * 6 + 1], 1.0 / 1024)
            for tc in range(4):
                p.wait(f"a_sq_{n}_{tc}")
                p.op("tensor_scalar_mul", c[:, tc * 6 + 1:tc * 6 + 2],
                     c[:, tc * 6 + 1:tc * 6 + 2], 1.0 / 1024)
            for tc in range(4):
                p.op("scalar_tensor_tensor", c[:, tc * 6 + 3:tc * 6 + 4],
                     c[:, tc * 6:tc * 6 + 1], -1.0, c[:, tc * 6:tc * 6 + 1],
                     ALU.mult, ALU.mult)
            for tc in range(4):
                p.op("tensor_tensor", c[:, tc * 6 + 3:tc * 6 + 4],
                     c[:, tc * 6 + 1:tc * 6 + 2], c[:, tc * 6 + 3:tc * 6 + 4],
                     ALU.add, inc=(s_dv, 1), label=f"a_var_{n}_{tc}")

        ln1_stats(p, 0)
        for nt in range(NT_K):
            c = ln1c
            for tc in range(4):
                p.wait(f"a_sqv_{nt}_{tc}")
                p.op("reciprocal", c[:, tc * 6 + 2:tc * 6 + 3],
                     c[:, tc * 6 + 4:tc * 6 + 5])
            for tc in range(4):
                p.op("scalar_tensor_tensor", c[:, tc * 6 + 5:tc * 6 + 6],
                     c[:, tc * 6:tc * 6 + 1], -1.0, c[:, tc * 6 + 2:tc * 6 + 3],
                     ALU.mult, ALU.mult, inc=(s_dv, 1), label=f"a_nmr_{nt}_{tc}")
            for tc in range(4):
                xt = bf(A_XBF[nt % 2] + tc * 1024, 1024)
                if nt >= 1:
                    p.wait(f"a_trx_{nt - 1}_{tc}")
                xlt = bf(A_XLT + tc * 1024, 1024)
                p.op("tensor_scalar", xlt, xt, c[:, tc * 6 + 2:tc * 6 + 3],
                     c[:, tc * 6 + 5:tc * 6 + 6], ALU.mult, ALU.add,
                     inc=(s_dv, 1), label=f"a_xln_{nt}_{tc}")
                if tc == 3:
                    prog.set_label(f"a_xdve_{nt}", s_dv, s_dv.n)
            for tc in range(4):
                p.wait(f"a_trx_{nt}_{tc}")
                xlf = bf(A_XLF[nt % 2], 4096)
                p.op("tensor_copy",
                     xlf.rearrange("p (cc t) -> p cc t", cc=8)[:, :, tc * 128:(tc + 1) * 128],
                     PST[:, :].rearrange("p (cc t) -> p cc t", cc=8),
                     inc=(s_dv, 1), label=f"a_xlf_{nt}_{tc}")
            if nt + 1 < NT_K:
                ln1_stats(p, nt + 1)
            for w in ("k", "v", "q"):
                if w == "q" and nt >= NT_Q:
                    continue
                for ob in range(8):
                    gi = gidx[(w, nt, ob)]
                    p.wait(f"a_mm_{gi}")
                    acc = acc_of(gi)
                    if w == "v":
                        jv = v_pos[gi]
                        if jv >= 2:
                            p.wait(f"a_vtr_{v_list[jv - 2]}")
                        slot = A_VFM + (jv % 2) * 512
                        p.op("tensor_copy", bf(slot, 512), acc,
                             inc=(s_dv, 1), label=f"a_ep_{gi}")
                        p.wait(f"a_vtr_{gi}")
                        vst = ABF[:, A_VST + (2 * ob) * 260:A_VST + (2 * ob + 2) * 260]
                        dst = vst.rearrange("p (hh tc j) -> p hh tc j",
                                            hh=2, tc=4)[:, :, :, 0:64]
                        srcv = PST[:, (jv % 2) * 512:(jv % 2) * 512 + 512].rearrange(
                            "p (tc hh j) -> p hh tc j", tc=4, hh=2)
                        p.op("tensor_copy", dst, srcv, inc=(s_dv, 1),
                             label=f"a_vev_{gi}")
                    else:
                        jw = w_pos[w][gi]
                        p.wait(f"a_acol_{w}")
                        if jw >= 4:
                            p.wait(f"a_st_{w_lists[w][jw - 1]}")
                        st = bf((A_KST if w == "k" else A_QST) + (jw % 4) * 512, 512)
                        p.op("tensor_scalar", st, acc, alpha[w][:, ob:ob + 1],
                             bcol[w][:, ob:ob + 1], ALU.mult, ALU.add,
                             inc=(s_dv, 1), label=f"a_ep_{gi}")
                        jkq = kq_pos[gi]
                        if jkq >= 1:
                            pg = kq_list[jkq - 1]
                            pw, pnt, pob = a_groups[pg]
                            p.wait(f"a_nrow_{pg}")
                            base = 16 if pw == "k" else 0
                            h0, h1 = 2 * pob, 2 * pob + 1
                            p.op("tensor_reduce", RT[0:1, 0:1], YPS[0:1, 0:512],
                                 AX.X, ALU.max)
                            p.op("tensor_reduce", RT[0:1, 1:2], YPS[0:1, 512:1024],
                                 AX.X, ALU.max)
                            p.op("tensor_tensor", NRM[0:1, base + h0:base + h0 + 1],
                                 NRM[0:1, base + h0:base + h0 + 1], RT[0:1, 0:1],
                                 ALU.max)
                            p.op("tensor_tensor", NRM[0:1, base + h1:base + h1 + 1],
                                 NRM[0:1, base + h1:base + h1 + 1], RT[0:1, 1:2],
                                 ALU.max, inc=(s_dv, 1), label=f"a_nre_{pg}")
                            prog.set_label(f"a_nmax_{pg}", s_dv, s_dv.n)
        pg = kq_list[-1]
        pw, pnt, pob = a_groups[pg]
        p.wait(f"a_nrow_{pg}")
        base = 16 if pw == "k" else 0
        h0, h1 = 2 * pob, 2 * pob + 1
        p.op("tensor_reduce", RT[0:1, 0:1], YPS[0:1, 0:512], AX.X, ALU.max)
        p.op("tensor_reduce", RT[0:1, 1:2], YPS[0:1, 512:1024], AX.X, ALU.max)
        p.op("tensor_tensor", NRM[0:1, base + h0:base + h0 + 1],
             NRM[0:1, base + h0:base + h0 + 1], RT[0:1, 0:1], ALU.max)
        p.op("tensor_tensor", NRM[0:1, base + h1:base + h1 + 1],
             NRM[0:1, base + h1:base + h1 + 1], RT[0:1, 1:2], ALU.max,
             inc=(s_dv, 1), label=f"a_nre_{pg}")
        p.op("memset", NROWB[0:1, 0:512], 0.0)
        p.op("memset", NROWB[0:1, 512:1024], 0.0, inc=(s_dv, 1),
             label="a_nrm_done")
        p.wait("a_mhsq")
        p.op("tensor_tensor", NRM[0:1, 32:48], NRM[0:1, 0:16], NRM[0:1, 16:32],
             ALU.mult)
        p.op("memset", NROWB[0:1, 0:8], 0.0)
        p.op("memset", NROWB[0:1, 8:16], 0.0)
        pe_ = NRM[0:1, 32:48].rearrange("a (pr two) -> a pr two", two=2)
        p.op("tensor_reduce", RT[0:1, 0:8], pe_, AX.X, ALU.max)
        p.op("memset", NROWB[0:1, 16:24], 0.0)
        p.op("memset", NROWB[0:1, 24:32], 0.0)
        p.op("tensor_scalar_mul", NRM[0:1, 32:40], RT[0:1, 0:8], -1.0)
        p.op("memset", NROWB[0:1, 32:40], 0.0)
        p.op("memset", NROWB[0:1, 40:48], 0.0, inc=(s_dv, 1),
             label="a_mh_rows")
        p.wait("a_mhb")
        p.op("tensor_copy", MHC[:], PSR[:, 16:24], inc=(s_dv, 1), label="a_mhc")

    def a_tensor(p):
        prevw = None
        for w in "qkv":
            p.wait(f"a_abs_{w}")
            if prevw is not None:
                p.wait(f"a_acol_{prevw}")
            ab = bf(A_WABS, 8192)
            for ob in range(8):
                for cc in range(8):
                    last = (ob == 7 and cc == 7)
                    p.op("matmul", PSR[:, ob:ob + 1],
                         ab[:, cc * 1024 + ob * 128:cc * 1024 + (ob + 1) * 128],
                         ones_b[:], start=(cc == 0), stop=(cc == 7),
                         inc=((s_pe, 1) if last else ()),
                         label=(f"a_alpha_{w}" if last else None))
            prevw = w
        pend = [None]

        def flush_aux():
            if pend[0] is None:
                return
            if pend[0][0] == "kq":
                _, gi, w_, _pnt = pend[0]
                jkq = kq_pos[gi]
                p.wait(f"a_nsq_{gi}")
                if jkq >= 1:
                    p.wait(f"a_nre_{kq_list[jkq - 1]}")
                jw = w_pos[w_][gi]
                srcn = bf((A_JUNK if w_ == "k" else A_ABSQ)
                          + (jw % 2) * 512, 512)
                p.op("matmul", YPS[0:1, 0:512], ones_b[0:64, :],
                     srcn[0:64, :], start=True, stop=True)
                p.op("matmul", YPS[0:1, 512:1024], ones_b[64:128, :],
                     srcn[64:128, :], start=True, stop=True,
                     inc=(s_pe, 1), label=f"a_nrow_{gi}")
            else:
                _, gi, vnt = pend[0]
                jv = v_pos[gi]
                p.wait(f"a_ep_{gi}")
                if jv >= 2:
                    p.wait(f"a_vev_{v_list[jv - 2]}")
                if vnt >= 1:
                    p.wait(f"a_vdma_{vnt - 1}")
                vfm = bf(A_VFM + (jv % 2) * 512, 512)
                for tcl in range(4):
                    p.op("transpose",
                         PST[:, (jv % 2) * 512 + tcl * 128:
                             (jv % 2) * 512 + (tcl + 1) * 128],
                         vfm[:, tcl * 128:(tcl + 1) * 128], idbf[:],
                         inc=((s_pe, 1) if tcl == 3 else ()),
                         label=(f"a_vtr_{gi}" if tcl == 3 else None))
            pend[0] = None
        for nt in range(NT_K):
            if pend[0] is not None and pend[0][0] == "v":
                flush_aux()
            for tc in range(4):
                p.wait(f"a_xln_{nt}_{tc}")
                if nt > 0 or tc > 0:
                    pnt, ptc = (nt, tc - 1) if tc > 0 else (nt - 1, 3)
                    p.wait(f"a_xlf_{pnt}_{ptc}")
                if nt > 0 and tc == 0:
                    p.wait(f"a_vev_{gidx[('v', nt - 1, 6)]}")
                    p.wait(f"a_vev_{gidx[('v', nt - 1, 7)]}")
                xlt = bf(A_XLT + tc * 1024, 1024)
                for cc in range(8):
                    p.op("transpose", PST[:, cc * 128:(cc + 1) * 128],
                         xlt[:, cc * 128:(cc + 1) * 128], idbf[:],
                         inc=((s_pe, 1) if cc == 7 else ()),
                         label=(f"a_trx_{nt}_{tc}" if cc == 7 else None))
            p.wait(f"a_xlf_{nt}_3")
            xlf = bf(A_XLF[nt % 2], 4096)
            for w in ("k", "v", "q"):
                if w == "q" and nt >= NT_Q:
                    continue
                p.wait(f"a_sgn_{w}")
                sgn = bf(A_WSGN[w], 8192)
                for ob in range(8):
                    gi = gidx[(w, nt, ob)]
                    if gi >= 4:
                        p.wait(f"a_ep_{gi - 4}")
                    acc = acc_of(gi)
                    for cc in range(8):
                        p.op("matmul", acc,
                             sgn[:, cc * 1024 + ob * 128:cc * 1024 + (ob + 1) * 128],
                             xlf[:, cc * 512:(cc + 1) * 512],
                             start=(cc == 0), stop=(cc == 7),
                             inc=((s_pe, 1) if cc == 7 else ()),
                             label=(f"a_mm_{gi}" if cc == 7 else None))
                    flush_aux()
                    if w in ("k", "q"):
                        pend[0] = ("kq", gi, w, nt)
                    else:
                        pend[0] = ("v", gi, nt)
        flush_aux()
        p.wait("a_mh_rows")
        for pr in range(8):
            p.op("matmul", PSR[:, 16 + pr:17 + pr], ones_f[:],
                 NRM[0:1, 32 + pr:33 + pr], start=True, stop=True,
                 inc=((s_pe, 1) if pr == 7 else ()),
                 label=("a_mhb" if pr == 7 else None))

    def a_sync(p):
        for nt in range(NT_K):
            for w in ("k", "q"):
                if w == "q" and nt >= NT_Q:
                    continue
                for ob in range(8):
                    gi = gidx[(w, nt, ob)]
                    jw = w_pos[w][gi]
                    p.wait(f"a_nsq_{gi}")
                    st = bf((A_KST if w == "k" else A_QST) + (jw % 4) * 512, 512)
                    dst = (k_scr if w == "k" else q_scr)[
                        ob * 128:(ob + 1) * 128, nt * 512:(nt + 1) * 512]
                    p.dma(dst, st, s_stk if w == "k" else s_stq,
                          label=f"a_st_{gi}")
            p.wait(f"a_vev_{gidx[('v', nt, 6)]}")
            p.wait(f"a_vev_{gidx[('v', nt, 7)]}")
            p.dma(v_scr[nt, :, :], bf(A_VST, 4160), s_vst, label=f"a_vdma_{nt}")
        if DEBUG:
            p.wait("a_mhc")
            p.dma(mh_dbg, NRM[0:1, :], s_ds)
        p.wait_sem(s_ds, s_ds.n)

    blocks.append(("a", {"gpsimd": a_gpsimd, "scalar": a_scalar,
                         "vector": a_vector, "tensor": a_tensor,
                         "sync": a_sync}))

    pc_slices = []
    for ob in range(32):
        pc_slices.append(("w1", ob))
        pc_slices.append(("w2", ob))
    for ob in range(32):
        pc_slices.append(("h1", ob))
    for ob in range(32):
        pc_slices.append(("h2", ob))
    for ob in range(8):
        pc_slices.append(("w3", ob))
    PC_SRC = {"w1": (w1T, 8, w1b), "w2": (w2T, 8, w2b),
              "h1": (h1T, 32, h1b), "h2": (h2T, 32, h2b), "w3": (w3T, 32, w3b)}
    PC_STG = [40960, 45056, 49152, 53248]

    if PH_B:
        NCH = 128

        def b_emit_loads(p, pr):
            if pr >= 2:
                p.wait(f"b_s_{(pr - 2) * NCH + NCH - 1}")
                p.wait(f"b_av_{(pr - 2) * NCH + NCH - 1}")
            p.dma(bf(B_QH[pr % 2], 2048), q_scr[pr * 128:(pr + 1) * 128, :],
                  s_db[pr % 2])
            p.dma(bf(B_KP[pr % 2], 4096), k_scr[pr * 128:(pr + 1) * 128, :],
                  s_db[pr % 2])
            for hh in range(2):
                h = 2 * pr + hh
                srcv = v_scr.rearrange("nt p (h x) -> p nt h x", h=16)[
                    :, :, h, :]
                p.dma(bf(B_VH[(pr % 2) * 2 + hh], 2080).rearrange(
                    "p (nt x) -> p nt x", nt=8), srcv, s_db[pr % 2],
                    label=(f"b_dma_{pr}" if hh == 1 else None))

        def b_gpsimd(p):
            npc = len(pc_slices)
            for ipc in range(npc):
                w, ob = pc_slices[ipc]
                wt_, ncc, dst = PC_SRC[w]
                if ipc >= 4:
                    p.wait(f"b_pco_{ipc - 4}")
                srcw = wt_[:, ob * 128:(ob + 1) * 128].rearrange(
                    "(cc p) o -> p cc o", p=128)
                p.dma(bf(PC_STG[ipc % 4], ncc * 128).rearrange(
                    "p (cc o) -> p cc o", cc=ncc), srcw, s_pc[ipc % 4],
                    label=f"b_pcc_{ipc}")
            for sx in s_pc:
                p.wait_sem(sx, sx.n)

        def b_emit_s(p, pr, ci):
            qt, kc = divmod(ci, 32)
            g = pr * NCH + ci
            if g >= 2:
                p.wait(f"b_e_{g - 2}")
            if ci == 0:
                p.wait(f"b_dma_{pr}")
            qh = bf(B_QH[pr % 2], 2048)
            kp = bf(B_KP[pr % 2], 4096)
            ps = AB0 if g % 2 == 0 else AB1
            p.op("matmul", ps[:, 0:512], kp[0:64, kc * 128:(kc + 1) * 128],
                 qh[0:64, qt * 512:(qt + 1) * 512], start=True, stop=True,
                 tile_position=(0, 0))
            p.op("matmul", ps[:, 512:1024], kp[64:128, kc * 128:(kc + 1) * 128],
                 qh[64:128, qt * 512:(qt + 1) * 512], start=True, stop=True,
                 tile_position=(64, 0), inc=(s_pe, 1), label=f"b_s_{g}")

        def b_tensor(p):
            p.wait("a_mhc")
            for pr in range(8):
                if pr == 0:
                    b_emit_s(p, pr, 0)
                vA = bf(B_VH[(pr % 2) * 2], 2080)
                vB = bf(B_VH[(pr % 2) * 2 + 1], 2080)
                for ci in range(NCH):
                    qt, kc = divmod(ci, 32)
                    g = pr * NCH + ci
                    if ci + 1 < NCH:
                        b_emit_s(p, pr, ci + 1)
                    elif pr + 1 < 8:
                        b_emit_s(p, pr + 1, 0)
                    p.wait(f"b_e_{g}")
                    if kc == 0:
                        if qt > 0:
                            p.wait(f"b_ynn_{pr}_{qt - 1}_1")
                        elif pr > 0:
                            p.wait(f"b_ynn_{pr - 1}_3_1")
                    pt = bf(B_P[g % 2], 1024)
                    p.op("matmul", YPS[0:65, 0:512], vA[:, kc * 65:(kc + 1) * 65],
                         pt[:, 0:512], start=(kc == 0), stop=(kc == 31))
                    p.op("matmul", YPS[0:65, 512:1024], vB[:, kc * 65:(kc + 1) * 65],
                         pt[:, 512:1024], start=(kc == 0), stop=(kc == 31),
                         inc=(s_pe, 1), label=f"b_av_{g}")
                    if kc == 31:
                        p.wait(f"b_rc_{pr}_{qt}_0")
                        if pr > 0 or qt > 0:
                            pt = pr * 4 + qt - 1
                            p.wait(f"b_cpa_{pt // 4}_{pt % 4}")
                        p.op("matmul", PSR[0:64, :], ones_f[0:1, 0:64],
                             SINV[0:1, 0:512], start=True, stop=True,
                             inc=(s_pe, 1), label=f"b_bc_{pr}_{qt}_0")
                        p.op("matmul", PSR[64:128, :], ones_f[0:1, 0:64],
                             SINV[0:1, 512:1024], start=True, stop=True,
                             tile_position=(0, 64),
                             inc=(s_pe, 1), label=f"b_bc_{pr}_{qt}_1")


        def b_scalar(p):
            b_emit_loads(p, 0)
            b_emit_loads(p, 1)
            for pr in range(8):
                for ci in range(NCH):
                    if ci == 3 and 2 <= pr + 1 < 8:
                        b_emit_loads(p, pr + 1)
                    qt, kc = divmod(ci, 32)
                    g = pr * NCH + ci
                    p.wait(f"b_s_{g}")
                    if g >= 2:
                        p.wait(f"b_av_{g - 2}")
                    ps = AB0 if g % 2 == 0 else AB1
                    p.op("activation", bf(B_P[g % 2], 1024), ps[:, :], AF.Exp,
                         bias=MHC[:, pr:pr + 1], inc=(s_ac, 1), label=f"b_e_{g}")
            for sx in s_db:
                p.wait_sem(sx, sx.n)


        def b_vector(p):
            for pr in range(8):
                for qt in range(4):
                    g_last = pr * NCH + qt * 32 + 31
                    p.wait(f"b_av_{g_last}")
                    p.op("reciprocal", SINV[0:1, 0:512], YPS[64:65, 0:512])
                    p.op("reciprocal", SINV[0:1, 512:1024], YPS[64:65, 512:1024],
                         inc=(s_dv, 1), label=f"b_rc_{pr}_{qt}_0")
                    prog.set_label(f"b_rc_{pr}_{qt}_1", s_dv, s_dv.n)
                    p.wait(f"b_bc_{pr}_{qt}_1")
                    p.op("tensor_copy", SINVB[0:64, 0:512], PSR[0:64, :])
                    p.op("tensor_copy", SINVB[0:64, 512:1024], PSR[64:128, :],
                         inc=(s_dv, 1), label=f"b_cpa_{pr}_{qt}")
                    p.op("memset", RT[0:1, 0:8], 0.0)
                    p.op("tensor_tensor", YN[0:64, 0:512], SINVB[0:64, 0:512],
                         YPS[0:64, 0:512], ALU.mult)
                    p.op("tensor_tensor", YN[0:64, 512:1024],
                         SINVB[0:64, 512:1024], YPS[0:64, 512:1024], ALU.mult,
                         inc=(s_dv, 1), label=f"b_ynn_{pr}_{qt}_1")
                    prog.set_label(f"b_ynn_{pr}_{qt}_0", s_dv, s_dv.n)
                    t = pr * 4 + qt
                    if t >= 2:
                        tp, tq = divmod(t - 2, 4)
                        p.wait(f"b_yod_{tp}_{tq}")
                    dsty = ABF[0:64, YSLAB + pr * 2048 + qt * 512:
                               YSLAB + pr * 2048 + (qt + 1) * 512]
                    p.op("tensor_scalar", dsty, YN[0:64, 0:512],
                         alpha["v"][0:64, pr:pr + 1], bcol["v"][0:64, pr:pr + 1],
                         ALU.mult, ALU.add,
                         inc=(s_dv, 1), label=f"b_ysl_{pr}_{qt}_0")
                    p.op("tensor_scalar",
                         YODD[0:64, (t % 2) * 512:(t % 2 + 1) * 512],
                         YN[0:64, 512:1024],
                         alpha["v"][64:128, pr:pr + 1], bcol["v"][64:128, pr:pr + 1],
                         ALU.mult, ALU.add,
                         inc=(s_dv, 1), label=f"b_ysl_{pr}_{qt}_1")


        def b_sync(p):
            npc = len(pc_slices)
            ipc = 0

            def emit_pco(n):
                nonlocal ipc
                for _ in range(n):
                    if ipc >= npc:
                        return
                    w, ob = pc_slices[ipc]
                    wt_, ncc, dst = PC_SRC[w]
                    p.wait(f"b_pcc_{ipc}")
                    p.dma(dst[ob], bf(PC_STG[ipc % 4], ncc * 128), s_po[ipc % 4],
                          label=f"b_pco_{ipc}")
                    ipc += 1
            for pr in range(8):
                for qt in range(4):
                    t = pr * 4 + qt
                    for k6 in range(6):
                        if ipc < npc:
                            g = pr * NCH + qt * 32 + 5 * k6 + 1
                            p.wait(f"b_e_{g}")
                            emit_pco(1)
                    p.wait(f"b_ysl_{pr}_{qt}_1")
                    p.dma(ABF[64:128, YSLAB + pr * 2048 + qt * 512:
                              YSLAB + pr * 2048 + (qt + 1) * 512],
                          YODD[0:64, (t % 2) * 512:(t % 2 + 1) * 512],
                          s_yod, label=f"b_yod_{pr}_{qt}")
            emit_pco(npc)
            if DEBUG:
                p.wait("b_ysl_7_3_0")
                p.wait("b_yod_7_3")
                p.dma(y_dbg, ABF[:, YSLAB:YSLAB + 16384], s_ds)
            p.wait_sem(s_ds, s_ds.n)
            p.wait_sem(s_yod, s_yod.n)
            for sx in s_po:
                p.wait_sem(sx, sx.n)

        blocks.append(("b", {"gpsimd": b_gpsimd, "tensor": b_tensor,
                             "scalar": b_scalar, "vector": b_vector,
                             "sync": b_sync}))

    if PH_C:
        def c_gpsimd(p):
            src = wT["p"].rearrange("(cc p) o -> p cc o", p=128)
            p.dma(bf(C_WPSGN, 8192).rearrange("p (cc o) -> p cc o", cc=8),
                  src, s_dw, label="c_wp")
            p.wait_sem(s_dw, s_dw.n)

        def c_scalar(p):
            p.wait("c_wp")
            p.op("activation", bf(C_WPABS, 8192), bf(C_WPSGN, 8192), AF.Abs,
                 inc=(s_ac, 1), label="c_abs")
            p.op("activation", bf(C_WPSGN, 8192), bf(C_WPSGN, 8192), AF.Sign,
                 inc=(s_ac, 1), label="c_sgn")

            for hp in range(2):
                for ob in range(8):
                    p.wait(f"c_ep_{ob * 4 + 3}")
                    if not (hp == 0 and ob < 2):
                        pj = (hp, ob - 2) if ob >= 2 else (hp - 1, 6 + ob)
                        p.wait(f"c_st2_{pj[0]}_{pj[1]}")
                    p.op("activation", bf(C_JUNK + (ob % 2) * 1024, 1024),
                         bf(C_ATTN + ob * 2048 + hp * 1024, 1024), AF.Square,
                         inc=(s_ac, 1), label=f"c_sqs_{hp}_{ob}")
            p.wait("c_var2")
            p.op("activation", R_B[0:1, :], R_B[0:1, :], AF.Sqrt, bias=EPSC[0:1, 0:1],
                 inc=(s_ac, 1), label="c_sq2")
            def emit_sq3(n):
                p.wait(f"c_var3_{n}")
                p.op("activation", R_B[0:1, n * 512:(n + 1) * 512],
                     R_B[0:1, n * 512:(n + 1) * 512], AF.Sqrt,
                     bias=EPSC[0:1, 0:1], inc=(s_ac, 1), label=f"c_sq3_{n}")
            for nt in range(4):
                p.wait(f"c_r1_{nt}")
                if nt > 0:
                    p.wait(f"c_st3_{nt - 1}_{7}")
                for cc in range(8):
                    p.op("activation", bf(C_RB8 + cc * 512, 512),
                         AFP[:, F_R1 + cc * 512:F_R1 + (cc + 1) * 512], AF.Copy,
                         inc=(s_ac, 1))
                if nt > 0:
                    emit_sq3(nt - 1)
                for cc in range(8):
                    p.op("activation", bf(C_RBSQ8 + cc * 512, 512),
                         bf(C_RB8 + cc * 512, 512), AF.Square,
                         inc=(s_ac, 1), label=f"c_rbsq_{nt}_{cc}")
            emit_sq3(3)

        def c_tensor(p):
            p.wait("c_abs")
            ab = bf(C_WPABS, 8192)
            for ob in range(8):
                for cc in range(8):
                    last = (ob == 7 and cc == 7)
                    p.op("matmul", PSR[:, ob:ob + 1],
                         ab[:, cc * 1024 + ob * 128:cc * 1024 + (ob + 1) * 128],
                         ones_b[:], start=(cc == 0), stop=(cc == 7),
                         inc=((s_pe, 1) if last else ()),
                         label=("c_alpha" if last else None))
            p.wait("c_sgn")
            sgn = bf(C_WPSGN, 8192)
            ysl = ABF[:, YSLAB:YSLAB + 16384]
            for ob in range(8):
                for nt in range(4):
                    g = ob * 4 + nt
                    if g >= 3:
                        p.wait(f"c_ep_{g - 3}")
                    acc = [AB0[:, 0:512], AB0[:, 512:1024], AB1[:, 0:512]][g % 3]
                    for cc in range(8):
                        p.op("matmul", acc,
                             sgn[:, cc * 1024 + ob * 128:cc * 1024 + (ob + 1) * 128],
                             ysl[:, cc * 2048 + nt * 512:cc * 2048 + (nt + 1) * 512],
                             start=(cc == 0), stop=(cc == 7),
                             inc=((s_pe, 1) if cc == 7 else ()),
                             label=(f"c_mm_{g}" if cc == 7 else None))
            p.wait(f"c_ep_{31}")
            for hp in range(2):
                if hp > 0:
                    p.wait(f"c_sre2_{hp - 1}")
                for ob in range(8):
                    p.wait(f"c_sqs_{hp}_{ob}")
                    for ntl in range(2):
                        nt = hp * 2 + ntl
                        p.op("matmul", YPS[0:1, ntl * 512:(ntl + 1) * 512],
                             ones_b[:], bf(C_ATTN + ob * 2048 + nt * 512, 512),
                             start=(ob == 0), stop=(ob == 7))
                        p.op("matmul", AB0[0:1, ntl * 512:(ntl + 1) * 512],
                             ones_b[:],
                             bf(C_JUNK + (ob % 2) * 1024 + ntl * 512, 512),
                             start=(ob == 0), stop=(ob == 7),
                             inc=((s_pe, 1) if ntl == 1 else ()),
                             label=(f"c_st2_{hp}_{ob}" if ntl == 1 else None))
                prog.set_label(f"c_str2_{hp}", s_pe, s_pe.n)
            for nt in range(4):
                p.wait("c_nmr2")
                if nt > 0:
                    p.wait(f"c_b2e_{nt - 1}")
                p.op("matmul", PSR[:, 0:512], ones_f[:],
                     R_B[0:1, nt * 512:(nt + 1) * 512], start=True, stop=True,
                     inc=(s_pe, 1), label=f"c_ab2_{nt}")
                p.wait(f"c_a2e_{nt}")
                p.op("matmul", PSR[:, 0:512], ones_f[:],
                     R_A[0:1, nt * 512:(nt + 1) * 512], start=True, stop=True,
                     inc=(s_pe, 1), label=f"c_bb2_{nt}")
            def emit_bc3(n):
                p.wait(f"c_nmr3_{n}")
                if n > 0:
                    p.wait(f"c_b3e_{n - 1}")
                p.op("matmul", PSR[:, 0:512], ones_f[:],
                     R_B[0:1, n * 512:(n + 1) * 512], start=True, stop=True,
                     inc=(s_pe, 1), label=f"c_ab3_{n}")
                p.wait(f"c_a3e_{n}")
                p.op("matmul", PSR[:, 0:512], ones_f[:],
                     R_A[0:1, n * 512:(n + 1) * 512], start=True, stop=True,
                     inc=(s_pe, 1), label=f"c_bb3_{n}")
            xres_slots = [fp(F_XRES, 1024),
                          ABF[:, C_XRES1:C_XRES1 + 2048].bitcast(F32)]
            for nt in range(4):
                for tcl in range(4):
                    tc = nt * 4 + tcl
                    p.wait(f"c_xr_{tc}")
                    if tc >= 2:
                        p.wait(f"c_xfadd_{tc - 2}")
                    else:
                        p.wait("c_sre2_1")
                    if tcl == 0 and nt > 0:
                        p.wait(f"c_sre3_{nt - 1}")
                    xr = xres_slots[tc % 2]
                    tb = YPS if tc % 2 == 0 else AB0
                    for cc in range(8):
                        p.op("transpose", tb[:, cc * 128:(cc + 1) * 128],
                             xr[:, cc * 128:(cc + 1) * 128], id32[:],
                             inc=((s_pe, 1) if cc == 7 else ()),
                             label=(f"c_tr_{tc}" if cc == 7 else None))
                p.wait(f"c_xfadd_{nt * 4 + 3}")
                if nt > 0:
                    p.wait(f"c_sre3_{nt - 1}")
                for cc in range(8):
                    p.wait(f"c_rbsq_{nt}_{cc}")
                    p.op("matmul", YPS[0:1, 0:512], ones_b[:],
                         bf(C_RB8 + cc * 512, 512),
                         start=(cc == 0), stop=(cc == 7))
                    p.op("matmul", YPS[0:1, 512:1024], ones_b[:],
                         bf(C_RBSQ8 + cc * 512, 512),
                         start=(cc == 0), stop=(cc == 7),
                         inc=(s_pe, 1), label=f"c_st3_{nt}_{cc}")
                prog.set_label(f"c_str3_{nt}", s_pe, s_pe.n)
                if nt > 0:
                    emit_bc3(nt - 1)
            emit_bc3(3)

        def c_vector(p):
            p.wait("c_alpha")
            p.op("tensor_scalar_mul", alpha["p"][:], PSR[:, 0:8], 1.0 / 1024,
                 inc=(s_dv, 1), label="c_acol")
            for ob in range(8):
                for nt in range(4):
                    g = ob * 4 + nt
                    p.wait(f"c_mm_{g}")
                    acc = [AB0[:, 0:512], AB0[:, 512:1024], AB1[:, 0:512]][g % 3]
                    p.op("tensor_scalar", bf(C_ATTN + ob * 2048 + nt * 512, 512),
                         acc, alpha["p"][:, ob:ob + 1], bcol["p"][:, ob:ob + 1],
                         ALU.mult, ALU.add, inc=(s_dv, 1), label=f"c_ep_{g}")
            for hp in range(2):
                p.wait(f"c_str2_{hp}")
                for ntl in range(2):
                    nt = hp * 2 + ntl
                    p.op("tensor_scalar_mul", R_A[0:1, nt * 512:(nt + 1) * 512],
                         YPS[0:1, ntl * 512:(ntl + 1) * 512], 1.0 / 1024)
                    p.op("tensor_scalar_mul", R_B[0:1, nt * 512:(nt + 1) * 512],
                         AB0[0:1, ntl * 512:(ntl + 1) * 512], 1.0 / 1024,
                         inc=((s_dv, 1) if ntl == 1 else ()),
                         label=(f"c_sre2_{hp}" if ntl == 1 else None))
            for h2 in range(2):
                for j in (0, 1):
                    jj = h2 * 2 + j
                    p.op("tensor_tensor", NROWB[0:1, j * 512:(j + 1) * 512],
                         R_A[0:1, jj * 512:(jj + 1) * 512],
                         R_A[0:1, jj * 512:(jj + 1) * 512], ALU.mult)
                for j in (0, 1):
                    jj = h2 * 2 + j
                    p.op("tensor_tensor", R_B[0:1, jj * 512:(jj + 1) * 512],
                         R_B[0:1, jj * 512:(jj + 1) * 512],
                         NROWB[0:1, j * 512:(j + 1) * 512], ALU.subtract,
                         inc=((s_dv, 1) if jj == 3 else ()),
                         label=("c_var2" if jj == 3 else None))
            p.wait("c_sq2")
            p.op("reciprocal", R_B[0:1, :], R_B[0:1, :])
            p.op("memset", NROWB[0:1, :], 0.0)
            p.op("scalar_tensor_tensor", R_A[0:1, :], R_A[0:1, :], -1.0,
                 R_B[0:1, :], ALU.mult, ALU.mult)
            p.op("memset", NROWB[0:1, 0:512], 0.0)
            p.op("memset", NROWB[0:1, 512:1024], 0.0, inc=(s_dv, 1),
                 label="c_nmr2")
            for nt in range(4):
                p.wait(f"c_ab2_{nt}")
                p.op("tensor_copy", AFP[:, F_A2 + nt * 512:F_A2 + (nt + 1) * 512],
                     PSR[:, 0:512], inc=(s_dv, 1), label=f"c_a2e_{nt}")
                p.wait(f"c_bb2_{nt}")
                p.op("tensor_copy", AFP[:, F_B2 + nt * 512:F_B2 + (nt + 1) * 512],
                     PSR[:, 0:512], inc=(s_dv, 1), label=f"c_b2e_{nt}")
            def ln3_fin(n):
                p.wait(f"c_sq3_{n}")
                p.op("reciprocal", R_B[0:1, n * 512:(n + 1) * 512],
                     R_B[0:1, n * 512:(n + 1) * 512])
                p.op("memset", RT[0:1, 0:8], 0.0)
                p.op("scalar_tensor_tensor", R_A[0:1, n * 512:(n + 1) * 512],
                     R_A[0:1, n * 512:(n + 1) * 512], -1.0,
                     R_B[0:1, n * 512:(n + 1) * 512], ALU.mult, ALU.mult,
                     inc=(s_dv, 1), label=f"c_nmr3_{n}")
                p.wait(f"c_ab3_{n}")
                p.op("tensor_copy", AFP[:, F_A2 + n * 512:F_A2 + (n + 1) * 512],
                     PSR[:, 0:512], inc=(s_dv, 1), label=f"c_a3e_{n}")
                p.wait(f"c_bb3_{n}")
                p.op("tensor_copy", AFP[:, F_B2 + n * 512:F_B2 + (n + 1) * 512],
                     PSR[:, 0:512], inc=(s_dv, 1), label=f"c_b3e_{n}")
            for nt in range(4):
                if nt > 0:
                    p.wait(f"c_r1d_{nt - 1}")
                for cc in range(8):
                    at = bf(C_ATTN + cc * 2048 + nt * 512, 512)
                    r1c = AFP[:, F_R1 + cc * 512:F_R1 + (cc + 1) * 512]
                    p.op("tensor_tensor", r1c, at,
                         AFP[:, F_A2 + nt * 512:F_A2 + (nt + 1) * 512], ALU.mult)
                for cc in range(8):
                    r1c = AFP[:, F_R1 + cc * 512:F_R1 + (cc + 1) * 512]
                    p.op("tensor_tensor", r1c, r1c,
                         AFP[:, F_B2 + nt * 512:F_B2 + (nt + 1) * 512], ALU.add,
                         inc=((s_dv, 1) if cc == 7 else ()),
                         label=(f"c_pre_{nt}" if cc == 7 else None))
                for tcl in range(4):
                    tc = nt * 4 + tcl
                    p.wait(f"c_tr_{tc}")
                    tb = YPS if tc % 2 == 0 else AB0
                    for cc in range(8):
                        r1b = AFP[:, F_R1 + cc * 512 + tcl * 128:
                                  F_R1 + cc * 512 + (tcl + 1) * 128]
                        p.op("tensor_tensor", r1b, r1b,
                             tb[:, cc * 128:(cc + 1) * 128], ALU.add,
                             inc=((s_dv, 1) if cc == 7 else ()),
                             label=(f"c_xfadd_{tc}" if cc == 7 else None))
                prog.set_label(f"c_r1_{nt}", s_dv, s_dv.n)
                p.wait(f"c_str3_{nt}")
                p.op("tensor_scalar_mul", R_A[0:1, nt * 512:(nt + 1) * 512],
                     YPS[0:1, 0:512], 1.0 / 1024)
                p.op("tensor_scalar_mul", R_B[0:1, nt * 512:(nt + 1) * 512],
                     YPS[0:1, 512:1024], 1.0 / 1024,
                     inc=(s_dv, 1), label=f"c_sre3_{nt}")
                nsl = NROWB[0:1, (nt % 2) * 512:(nt % 2 + 1) * 512]
                p.op("tensor_tensor", nsl, R_A[0:1, nt * 512:(nt + 1) * 512],
                     R_A[0:1, nt * 512:(nt + 1) * 512], ALU.mult)
                p.op("memset", RT[0:1, 0:8], 0.0)
                p.op("tensor_tensor", R_B[0:1, nt * 512:(nt + 1) * 512],
                     R_B[0:1, nt * 512:(nt + 1) * 512], nsl, ALU.subtract,
                     inc=(s_dv, 1), label=f"c_var3_{nt}")
                if nt > 0:
                    ln3_fin(nt - 1)
            ln3_fin(3)
            for nt in range(4):
                p.wait(f"c_r1l_{nt}")
                for cc in range(8):
                    r1c = AFP[:, F_R1 + cc * 512:F_R1 + (cc + 1) * 512]
                    xfc = bf(C_XFN + cc * 512, 512)
                    p.op("tensor_tensor", xfc, r1c,
                         AFP[:, F_A2 + nt * 512:F_A2 + (nt + 1) * 512], ALU.mult)
                for cc in range(8):
                    xfc = bf(C_XFN + cc * 512, 512)
                    p.op("tensor_tensor", xfc, xfc,
                         AFP[:, F_B2 + nt * 512:F_B2 + (nt + 1) * 512], ALU.add,
                         inc=((s_dv, 1) if cc == 7 else ()),
                         label=(f"c_xfn_{nt}" if cc == 7 else None))

        def c_sync(p):
            for nt in range(4):
                for tcl in range(4):
                    tc = nt * 4 + tcl
                    if tc == 1:
                        p.wait("c_mm_31")
                    if tc >= 2:
                        p.wait(f"c_tr_{tc - 2}")
                    dst = [fp(F_XRES, 1024),
                           ABF[:, C_XRES1:C_XRES1 + 2048].bitcast(F32)][tc % 2]
                    p.dma(dst, x_in[tc * 128:(tc + 1) * 128, :], s_xr,
                          label=f"c_xr_{tc}")
                p.wait(f"c_r1_{nt}")
                p.wait(f"c_st3_{nt}_{7}")
                p.dma(r1_scr.rearrange("cc p t -> p cc t")[:, :, nt * 512:(nt + 1) * 512],
                      fp(F_R1, 4096).rearrange("p (cc t) -> p cc t", cc=8),
                      s_ds, label=f"c_r1d_{nt}")
            for nt in range(4):
                p.wait("c_nmr3_3")
                p.wait(f"c_r1d_{3}")
                if nt > 0:
                    p.wait(f"c_xfn_{nt - 1}")
                p.dma(fp(F_R1, 4096).rearrange("p (cc t) -> p cc t", cc=8),
                      r1_scr.rearrange("cc p t -> p cc t")[:, :, nt * 512:(nt + 1) * 512],
                      s_ds, label=f"c_r1l_{nt}")
                p.wait(f"c_xfn_{nt}")
                p.dma(xffn_scr.rearrange("cc p t -> p cc t")[:, :, nt * 512:(nt + 1) * 512],
                      bf(C_XFN, 4096).rearrange("p (cc t) -> p cc t", cc=8),
                      s_ds, label=f"c_xfd_{nt}")
            p.wait_sem(s_ds, s_ds.n)
            p.wait_sem(s_xr, s_xr.n)

        blocks.append(("c", {"gpsimd": c_gpsimd, "scalar": c_scalar,
                             "tensor": c_tensor, "vector": c_vector,
                             "sync": c_sync}))

    if PH_D:
        NTT = 4
        def wd_seq(tt):
            seq = [("xf", tt, 0)]
            for ob in range(HOB):
                for wi in range(2):
                    seq.append(("w12", tt, ob * 2 + wi))
            for li in range(2):
                for ob in range(HOB):
                    seq.append(("h", tt, li * 32 + ob))
            for ob in range(8):
                seq.append(("w3", tt, ob))
            return seq

        full_seq = []
        for tt in range(NTT):
            full_seq += wd_seq(tt)

        def consumer_label(item):
            kind, tt, j = item
            if kind == "xf":
                return f"d_mm12_{tt}_{1}"
            if kind == "w12":
                return f"d_mm12_{tt}_{j}"
            if kind == "h":
                li, ob = divmod(j, 32)
                return f"d_mmh_{tt}_{li}_{ob}"
            return f"d_mmw3_{tt}_{j}"

        hb_of = {}
        gate_of = {}
        nxf_list = [it for it in full_seq if it[0] != "xf"]
        npos = {}
        for j, it in enumerate(nxf_list):
            npos[it] = j
            hb_of[it] = j % 4
        for idx, item in enumerate(full_seq):
            if item[0] == "xf":
                gate_of[idx] = None
            else:
                j = npos[item]
                gate_of[idx] = nxf_list[j - 4] if j >= 4 else None

        def d_emit_wd(p, q, idx):
            kind, tt, j = full_seq[idx]
            if gate_of[idx] is not None:
                p.wait(consumer_label(gate_of[idx]))
            if kind == "xf":
                if tt > 0:
                    p.wait(f"d_mmh_{tt - 1}_1_31")
                p.dma(ABF[:, D_B:D_B + 4096].rearrange("p (cc t) -> p cc t", cc=8),
                      xffn_scr.rearrange("cc p t -> p cc t")[
                          :, :, tt * 512:(tt + 1) * 512],
                      s_dxf, label=f"d_xf_{tt}")
                return
            jn = npos[full_seq[idx]]
            sem = s_dd[(jn % 2) * 2 + ((jn // 2) % 2)]
            if kind == "w12":
                ob, wi = divmod(j, 2)
                srcb = (w1b if wi == 0 else w2b)[ob]
                p.dma(bf(D_HB[hb_of[full_seq[idx]]], 1024), srcb, sem,
                      label=f"d_wd_{tt}_w12_{j}")
            elif kind == "h":
                li, ob = divmod(j, 32)
                srcb = (h1b if li == 0 else h2b)[ob]
                p.dma(bf(D_HB[hb_of[full_seq[idx]]], 4096), srcb, sem,
                      label=f"d_wd_{tt}_h_{j}")
            else:
                p.dma(bf(D_HB[hb_of[full_seq[idx]]], 4096), w3b[j], sem,
                      label=f"d_wd_{tt}_w3_{j}")

        def q_of(idx):
            it = full_seq[idx]
            if it[0] == "xf":
                return 0
            return npos[it] % 2

        def d_gpsimd(p):
            for idx in range(len(full_seq)):
                if q_of(idx) == 1:
                    d_emit_wd(p, 1, idx)
            p.wait_sem(s_dd[2], s_dd[2].n)
            p.wait_sem(s_dd[3], s_dd[3].n)

        def d_tensor(p):
            for tt in range(NTT):
                p.wait(f"d_xf_{tt}")
                xf = ABF[:, D_B:D_B + 4096]
                for ob in range(HOB):
                    for wi in range(2):
                        g = ob * 2 + wi
                        p.wait(f"d_wd_{tt}_w12_{g}")
                        if g >= 4:
                            pg = g - 4
                            if pg % 2 == 0:
                                p.wait(f"d_t1_{tt}_{pg // 2}")
                            else:
                                p.wait(f"d_ep2_{tt}_{pg // 2}")
                        acc = acc_of(g)
                        wsl = bf(D_HB[hb_of[("w12", tt, g)]], 1024)
                        for cc in range(8):
                            p.op("matmul", acc, wsl[:, cc * 128:(cc + 1) * 128],
                                 xf[:, cc * 512:(cc + 1) * 512],
                                 start=(cc == 0), stop=(cc == 7),
                                 inc=((s_pe, 1) if cc == 7 else ()),
                                 label=(f"d_mm12_{tt}_{g}" if cc == 7 else None))
                for li in range(2):
                    src0 = D_A if li == 0 else D_B
                    for ob in range(HOB):
                        g = li * 32 + ob
                        p.wait(f"d_wd_{tt}_h_{g}")
                        if li == 0 and ob == 0:
                            p.wait(f"d_a_{tt}_{HOB - 1}")
                        if li == 1 and ob == 0:
                            p.wait(f"d_geldone_{tt}_0")
                        gg = ob % 4
                        if ob >= 4 or li == 1:
                            pli, pob = (li, ob - 4) if ob >= 4 else (li - 1, 28 + ob)
                            p.wait(f"d_eph_{tt}_{pli}_{pob}")
                        acc = acc_of(gg)
                        wsl = bf(D_HB[hb_of[("h", tt, g)]], 4096)
                        for cc in range(32):
                            p.op("matmul", acc, wsl[:, cc * 128:(cc + 1) * 128],
                                 ABF[:, src0 + cc * 512:src0 + (cc + 1) * 512],
                                 start=(cc == 0), stop=(cc == 31),
                                 inc=((s_pe, 1) if cc == 31 else ()),
                                 label=(f"d_mmh_{tt}_{li}_{ob}" if cc == 31 else None))
                for ob in range(8):
                    p.wait(f"d_wd_{tt}_w3_{ob}")
                    if ob == 0:
                        p.wait(f"d_geldone_{tt}_1")
                    if ob >= 4:
                        p.wait(f"d_epw3_{tt}_{ob - 4}")
                    acc = acc_of(ob % 4)
                    wsl = bf(D_HB[hb_of[("w3", tt, ob)]], 4096)
                    for cc in range(32):
                        p.op("matmul", acc, wsl[:, cc * 128:(cc + 1) * 128],
                             ABF[:, D_A + cc * 512:D_A + (cc + 1) * 512],
                             start=(cc == 0), stop=(cc == 31),
                             inc=((s_pe, 1) if cc == 31 else ()),
                             label=(f"d_mmw3_{tt}_{ob}" if cc == 31 else None))
                if tt > 0:
                    p.wait(f"d_oev_{tt - 1}_3")
                for ob in range(8):
                    p.wait(f"d_fsq_{tt}_{ob}")
                    p.op("matmul", YPS[0:1, 0:512], ones_b[:],
                         bf(D_FO_S[tt % 2] + ob * 512, 512),
                         start=(ob == 0), stop=(ob == 7))
                    p.op("matmul", YPS[0:1, 512:1024], ones_b[:],
                         bf(D_JK8 + ob * 512, 512), start=(ob == 0), stop=(ob == 7),
                         inc=(s_pe, 1), label=f"d_st4_{tt}_{ob}")
                prog.set_label(f"d_str4_{tt}", s_pe, s_pe.n)
                p.wait(f"d_nmr4_{tt}")
                p.op("matmul", PSR[:, 0:512], ones_f[:], R_B[0:1, 0:512],
                     start=True, stop=True, inc=(s_pe, 1), label=f"d_ab4_{tt}")
                p.wait(f"d_a4e_{tt}")
                p.op("matmul", PSR[:, 0:512], ones_f[:], R_A[0:1, 0:512],
                     start=True, stop=True, inc=(s_pe, 1), label=f"d_bb4_{tt}")
                for tcl in range(4):
                    p.wait(f"d_of_{tt}")
                    if tcl > 0:
                        p.wait(f"d_oev_{tt}_{tcl - 1}")
                    elif tt > 0:
                        p.wait(f"d_oev_{tt - 1}_3")
                    for cc in range(8):
                        p.op("transpose", YPS[:, cc * 128:(cc + 1) * 128],
                             AFP[:, F_R1 + cc * 512 + tcl * 128:
                                 F_R1 + cc * 512 + (tcl + 1) * 128], id32[:],
                             inc=((s_pe, 1) if cc == 7 else ()),
                             label=(f"d_tro_{tt}_{tcl}" if cc == 7 else None))

        def d_scalar(p):
            for tt in range(NTT):
                fo0 = D_FO_S[tt % 2]
                if tt > 1:
                    p.wait(f"d_of_{tt - 2}")
                for ob in range(HOB):
                    g1 = ob * 2
                    p.wait(f"d_mm12_{tt}_{g1}")
                    if ob >= 2:
                        p.wait(f"d_gate_{tt}_{ob - 2}")
                    acc = acc_of(g1)
                    p.op("activation", bf(fo0 + (ob % 2) * 512, 512), acc,
                         AF.Silu, scale=scol["s1"][:, ob:ob + 1],
                         inc=(s_ac, 1), label=f"d_t1_{tt}_{ob}")
                for li in range(2):
                    sc = scol["sh1"] if li == 0 else scol["sh2"]
                    dst0 = D_B if li == 0 else D_A
                    for ob in range(HOB):
                        p.wait(f"d_mmh_{tt}_{li}_{ob}")
                        acc = acc_of(ob % 4)
                        p.op("activation", bf(dst0 + ob * 512, 512), acc,
                             AF.Gelu, scale=sc[:, ob:ob + 1],
                             inc=(s_ac, 1), label=f"d_eph_{tt}_{li}_{ob}")
                    prog.set_label(f"d_geldone_{tt}_{li}", s_ac, s_ac.n)
                for ob in range(8):
                    p.wait(f"d_mmw3_{tt}_{ob}")
                    acc = acc_of(ob % 4)
                    p.op("activation", bf(fo0 + ob * 512, 512), acc, AF.Copy,
                         scale=s3col[:, ob:ob + 1],
                         inc=(s_ac, 1), label=f"d_epw3_{tt}_{ob}")
                for ob in range(8):
                    p.op("activation", bf(D_JK8 + ob * 512, 512),
                         bf(fo0 + ob * 512, 512),
                         AF.Square, inc=(s_ac, 1), label=f"d_fsq_{tt}_{ob}")
                p.wait(f"d_var4_{tt}")
                p.op("activation", R_B[0:1, 0:512], R_B[0:1, 0:512], AF.Sqrt,
                     bias=EPSC[0:1, 0:1], inc=(s_ac, 1), label=f"d_sq4_{tt}")
                p.wait(f"d_nmr4_{tt}")
                if tt > 0:
                    p.wait(f"d_tro_{tt - 1}_3")
                p.dma(fp(F_R1, 4096).rearrange("p (cc t) -> p cc t", cc=8),
                      r1_scr.rearrange("cc p t -> p cc t")[
                          :, :, tt * 512:(tt + 1) * 512],
                      s_ds, label=f"d_r1l_{tt}")
                for tcl in range(4):
                    tcg = tt * 4 + tcl
                    p.wait(f"d_oev_{tt}_{tcl}")
                    p.dma(out[tcg * 128:(tcg + 1) * 128, :],
                          fp(F_OUT[tcg % 2], 1024), s_out,
                          label=f"d_od_{tt}_{tcl}")
            p.wait_sem(s_out, s_out.n)
            p.wait_sem(s_ds, s_ds.n)

        def d_vector(p):
            for tt in range(NTT):
                fo0 = D_FO_S[tt % 2]

                def emit_gate(ob):
                    p.op("tensor_tensor",
                         ABF[:, D_A + ob * 512:D_A + (ob + 1) * 512],
                         bf(fo0 + (ob % 2) * 512, 512),
                         bf(fo0 + 1024 + (ob % 2) * 512, 512), ALU.mult,
                         inc=(s_dv, 1), label=f"d_gate_{tt}_{ob}")
                    prog.set_label(f"d_a_{tt}_{ob}", s_dv, s_dv.n)
                for ob in range(HOB):
                    g2 = ob * 2 + 1
                    p.wait(f"d_t1_{tt}_{ob}")
                    p.wait(f"d_mm12_{tt}_{g2}")
                    if tt > 1 and ob == 0:
                        p.wait(f"d_of_{tt - 2}")
                    acc2 = acc_of(g2)
                    p.op("tensor_scalar", bf(fo0 + 1024 + (ob % 2) * 512, 512),
                         acc2, scol["s2"][:, ob:ob + 1], None, ALU.mult, ALU.bypass,
                         inc=(s_dv, 1), label=f"d_ep2_{tt}_{ob}")
                    if ob > 0:
                        emit_gate(ob - 1)
                emit_gate(HOB - 1)
                p.wait(f"d_str4_{tt}")
                p.op("tensor_scalar_mul", R_A[0:1, 0:512], YPS[0:1, 0:512],
                     1.0 / 1024)
                p.op("tensor_scalar_mul", R_B[0:1, 0:512], YPS[0:1, 512:1024],
                     1.0 / 1024, inc=(s_dv, 1), label=f"d_sre4_{tt}")
                p.op("tensor_tensor", NROWB[0:1, 0:512], R_A[0:1, 0:512],
                     R_A[0:1, 0:512], ALU.mult)
                p.op("memset", NROWB[0:1, 512:1024], 0.0)
                p.op("tensor_tensor", R_B[0:1, 0:512], R_B[0:1, 0:512],
                     NROWB[0:1, 0:512], ALU.subtract)
                p.op("memset", NROWB[0:1, 512:768], 0.0)
                p.op("memset", NROWB[0:1, 768:1024], 0.0, inc=(s_dv, 1),
                     label=f"d_var4_{tt}")
                p.wait(f"d_sq4_{tt}")
                p.op("reciprocal", R_B[0:1, 0:512], R_B[0:1, 0:512])
                p.op("memset", NROWB[0:1, 0:512], 0.0)
                p.op("scalar_tensor_tensor", R_A[0:1, 0:512], R_A[0:1, 0:512],
                     -1.0, R_B[0:1, 0:512], ALU.mult, ALU.mult)
                p.op("memset", NROWB[0:1, 0:256], 0.0)
                p.op("memset", NROWB[0:1, 256:512], 0.0, inc=(s_dv, 1),
                     label=f"d_nmr4_{tt}")
                p.wait(f"d_ab4_{tt}")
                p.op("tensor_copy", fp(F_A4, 512), PSR[:, 0:512],
                     inc=(s_dv, 1), label=f"d_a4e_{tt}")
                p.wait(f"d_bb4_{tt}")
                p.op("tensor_copy", fp(F_B4, 512), PSR[:, 0:512],
                     inc=(s_dv, 1), label=f"d_b4e_{tt}")
                p.wait(f"d_r1l_{tt}")
                for cc in range(8):
                    r1c = AFP[:, F_R1 + cc * 512:F_R1 + (cc + 1) * 512]
                    p.op("tensor_tensor", r1c, r1c, fp(F_B4, 512), ALU.add)
                for cc in range(8):
                    foc = bf(fo0 + cc * 512, 512)
                    scr = fp(F_SCR + (cc % 2) * 512, 512)
                    p.op("tensor_tensor", scr, foc, fp(F_A4, 512), ALU.mult)
                    if cc > 0:
                        pscr = fp(F_SCR + ((cc - 1) % 2) * 512, 512)
                        r1p = AFP[:, F_R1 + (cc - 1) * 512:F_R1 + cc * 512]
                        p.op("tensor_tensor", r1p, r1p, pscr, ALU.add)
                p.op("tensor_tensor", AFP[:, F_R1 + 7 * 512:F_R1 + 8 * 512],
                     AFP[:, F_R1 + 7 * 512:F_R1 + 8 * 512],
                     fp(F_SCR + 512, 512), ALU.add,
                     inc=(s_dv, 1), label=f"d_of_{tt}")
                for tcl in range(4):
                    p.wait(f"d_tro_{tt}_{tcl}")
                    tcg = tt * 4 + tcl
                    if tcg >= 1:
                        p.wait(f"d_od_{(tcg - 1) // 4}_{(tcg - 1) % 4}")
                    p.op("tensor_copy", fp(F_OUT[tcg % 2], 1024), YPS[:, :],
                         inc=(s_dv, 1), label=f"d_oev_{tt}_{tcl}")

        def d_sync(p):
            for idx in range(len(full_seq)):
                if q_of(idx) == 0:
                    d_emit_wd(p, 0, idx)
            p.wait_sem(s_dd[0], s_dd[0].n)
            p.wait_sem(s_dd[1], s_dd[1].n)
            p.wait_sem(s_dxf, s_dxf.n)
            p.wait_sem(s_out, s_out.n)

        blocks.append(("d", {"gpsimd": d_gpsimd, "tensor": d_tensor,
                             "scalar": d_scalar, "vector": d_vector,
                             "sync": d_sync}))

    run_blocks(nc, prog, blocks)
    return nc


_NC_CACHE = None
_LAST_RES = None


def kernel(**inputs):
    global _NC_CACHE
    x = np.ascontiguousarray(np.asarray(inputs["L0_attn_x"], dtype=np.float32))
    B, T, C = x.shape
    t = lambda a: np.ascontiguousarray(np.asarray(a, dtype=np.float32).T)
    r8 = lambda a: np.ascontiguousarray(np.asarray(a, dtype=np.float32).reshape(8, 128))
    r32 = lambda a: np.ascontiguousarray(np.asarray(a, dtype=np.float32).reshape(32, 128))
    shared = {
        "wqT": t(inputs["L0_attn_Wq"]), "wkT": t(inputs["L0_attn_Wk"]),
        "wvT": t(inputs["L0_attn_Wv"]), "wpT": t(inputs["L0_attn_Wp"]),
        "bq": r8(inputs["L0_attn_bq"]), "bk": r8(inputs["L0_attn_bk"]),
        "bv": r8(inputs["L0_attn_bv"]), "bp": r8(inputs["L0_attn_bp"]),
        "w1T": t(inputs["L0_ffn_w1_w"]), "w2T": t(inputs["L0_ffn_w2_w"]),
        "h1T": t(inputs["L0_ffn_h1_w"]), "h2T": t(inputs["L0_ffn_h2_w"]),
        "w3T": t(inputs["L0_ffn_w3_w"]),
        "s1": r32(inputs["L0_ffn_w1_s"]), "s2": r32(inputs["L0_ffn_w2_s"]),
        "sh1": r32(inputs["L0_ffn_h1_s"]), "sh2": r32(inputs["L0_ffn_h2_s"]),
        "s3": r8(inputs["L0_ffn_w3_s"]),
    }
    in_maps = []
    for b in range(4):
        for half in range(2):
            xr = np.ascontiguousarray(
                np.concatenate([x[b, half * 2048:], x[b, :half * 2048]], axis=0))
            m = dict(shared)
            m["x"] = xr
            in_maps.append(m)
    if _NC_CACHE is None:
        _NC_CACHE = build_nc()
    res = run_bass_kernel_spmd(_NC_CACHE, in_maps, core_ids=list(range(8)))
    global _LAST_RES
    _LAST_RES = res
    outp = np.empty((4, T, C), dtype=np.float32)
    for b in range(4):
        for half in range(2):
            outp[b, half * 2048:half * 2048 + 2048] = res.results[2 * b + half]["out"]
    return outp


if __name__ == "__main__":
    nc = build_nc()
    print("built ok, instructions:",
          sum(1 for _ in nc.all_instructions()) if hasattr(nc, "all_instructions") else "?")


# version 12
# speedup vs baseline: 1.1312x; 1.1312x over previous
import numpy as np
from contextlib import ExitStack

import concourse.bass as bass
import concourse.mybir as mybir
from concourse.bass_utils import run_bass_kernel_spmd

F32 = mybir.dt.float32
BF16 = mybir.dt.bfloat16
AF = mybir.ActivationFunctionType
ALU = mybir.AluOpType
AX = mybir.AxisListType

DEBUG = False
PH_B = True
PH_C = True
PH_D = True

D = 1024
TQ = 2048
TK = 4096
NT_K = 8
NT_Q = 4
FFD = 4096
HOB = 32


class Sem:
    def __init__(self, handle, name):
        self.h = handle
        self.name = name
        self.n = 0


class Prog:
    def __init__(self):
        self.labels = {}
        self.emit = False
        self.sems = []

    def mksem(self, nc, ctx, name):
        s = Sem(ctx.enter_context(nc.semaphore(name)), name)
        self.sems.append(s)
        return s

    def reset(self):
        for s in self.sems:
            s.n = 0

    def set_label(self, name, s, v):
        if self.emit:
            assert self.labels.get(name) == (s, v), \
                f"label drift {name}: {self.labels.get(name)} vs {(s, v)}"
        else:
            assert name not in self.labels, f"dup label {name}"
            self.labels[name] = (s, v)


class Px:
    def __init__(self, prog, name):
        self.prog = prog
        self.name = name
        self.eng = None

    def op(self, meth, *a, inc=(), label=None, **kw):
        if inc and isinstance(inc[0], Sem):
            inc = [inc]
        inst = None
        if self.prog.emit:
            inst = getattr(self.eng, meth)(*a, **kw)
            for (s, amt) in inc:
                inst.then_inc(s.h, amt)
        for (s, amt) in inc:
            s.n += amt
        if label is not None:
            assert inc, f"label {label} without inc"
            s, _ = inc[0]
            self.prog.set_label(label, s, s.n)
        return inst

    def dma(self, out, in_, sem, label=None, **kw):
        return self.op("dma_start", out, in_, inc=(sem, 16), label=label, **kw)

    def wait(self, label, delta=0):
        if self.prog.emit:
            s, v = self.prog.labels[label]
            self.eng.wait_ge(s.h, v + delta)

    def wait_sem(self, s, v):
        if self.prog.emit:
            self.eng.wait_ge(s.h, v)


ENGS = ["sync", "gpsimd", "tensor", "vector", "scalar"]


def run_blocks(nc, prog, blocks):
    pxs = {e: Px(prog, e) for e in ENGS}
    for emit in (False, True):
        prog.emit = emit
        prog.reset()
        for bname, engs in blocks:
            if not emit:
                for e in ENGS:
                    if e in engs:
                        engs[e](pxs[e])
            else:
                with nc.Block() as blk:
                    for e in ENGS:
                        if e in engs:
                            px, fn = pxs[e], engs[e]

                            def mk(fn=fn, px=px):
                                def _(engine):
                                    px.eng = engine
                                    fn(px)
                                return _
                            getattr(blk, e)(mk())


def build_nc():
    import ml_dtypes
    nc = bass.Bass("TRN2", target_bir_lowering=False,
                   detect_race_conditions=False)

    x_in = nc.dram_tensor("x", [TK, D], F32, kind="ExternalInput").ap()
    wT = {w: nc.dram_tensor(f"w{w}T", [D, D], F32, kind="ExternalInput").ap()
          for w in "qkvp"}
    bvec = {w: nc.dram_tensor(f"b{w}", [8, 128], F32, kind="ExternalInput").ap()
            for w in "qkvp"}
    w1T = nc.dram_tensor("w1T", [D, FFD], F32, kind="ExternalInput").ap()
    w2T = nc.dram_tensor("w2T", [D, FFD], F32, kind="ExternalInput").ap()
    h1T = nc.dram_tensor("h1T", [FFD, FFD], F32, kind="ExternalInput").ap()
    h2T = nc.dram_tensor("h2T", [FFD, FFD], F32, kind="ExternalInput").ap()
    w3T = nc.dram_tensor("w3T", [FFD, D], F32, kind="ExternalInput").ap()
    sc_in = {"s1": nc.dram_tensor("s1", [32, 128], F32, kind="ExternalInput").ap(),
             "s2": nc.dram_tensor("s2", [32, 128], F32, kind="ExternalInput").ap(),
             "sh1": nc.dram_tensor("sh1", [32, 128], F32, kind="ExternalInput").ap(),
             "sh2": nc.dram_tensor("sh2", [32, 128], F32, kind="ExternalInput").ap()}
    s3_in = nc.dram_tensor("s3", [8, 128], F32, kind="ExternalInput").ap()

    out = nc.dram_tensor("out", [TQ, D], F32, kind="ExternalOutput").ap()

    skind = "ExternalOutput" if DEBUG else "Internal"
    q_scr = nc.dram_tensor("q_scr", [D, TQ], BF16, kind=skind).ap()
    k_scr = nc.dram_tensor("k_scr", [D, TK], BF16, kind=skind).ap()
    v_scr = nc.dram_tensor("v_scr", [NT_K, 128, 4160], BF16, kind=skind).ap()
    y_dbg = nc.dram_tensor("y_dbg", [128, 8 * TQ], BF16, kind=skind).ap()
    mh_dbg = nc.dram_tensor("mh_dbg", [1, 48], F32, kind=skind).ap()
    r1_scr = nc.dram_tensor("r1_scr", [8, 128, TQ], F32, kind=skind).ap()
    xffn_scr = nc.dram_tensor("xffn_scr", [8, 128, TQ], BF16, kind=skind).ap()
    w1b = nc.dram_tensor("w1b", [32, 128, 1024], BF16, kind="Internal").ap()
    w2b = nc.dram_tensor("w2b", [32, 128, 1024], BF16, kind="Internal").ap()
    h1b = nc.dram_tensor("h1b", [32, 128, 4096], BF16, kind="Internal").ap()
    h2b = nc.dram_tensor("h2b", [32, 128, 4096], BF16, kind="Internal").ap()
    w3b = nc.dram_tensor("w3b", [8, 128, 4096], BF16, kind="Internal").ap()

    id32_d = nc.inline_tensor(np.eye(128, dtype=np.float32), name="id32c").ap()
    idbf_d = nc.inline_tensor(np.eye(128, dtype=ml_dtypes.bfloat16), name="idbfc").ap()

    prog = Prog()
    ctx = ExitStack()
    nc._kernel_ctx = ctx

    sbt = lambda n, s, dt: ctx.enter_context(nc.sbuf_tensor(n, s, dt))
    psa = lambda n, s, dt: ctx.enter_context(nc.psum_tensor(n, s, dt))

    ABF = sbt("ABF", [128, 66560], BF16)
    AFP = sbt("AFP", [128, 9216], F32)

    id32 = sbt("id32", [128, 128], F32)
    idbf = sbt("idbf", [128, 128], BF16)
    ones_f = sbt("ones_f", [1, 128], F32)
    ones_b = sbt("ones_b", [128, 1], BF16)
    alpha = {w: sbt(f"alpha_{w}", [128, 8], F32) for w in "qkvp"}
    bcol = {w: sbt(f"bcol_{w}", [128, 8], F32) for w in "qkvp"}
    scol = {n: sbt(f"scol_{n}", [128, 32], F32) for n in ("s1", "s2", "sh1", "sh2")}
    s3col = sbt("s3col", [128, 8], F32)
    ln1c = sbt("ln1c", [128, 24], F32)
    MHC = sbt("MHC", [128, 8], F32)
    NRM = sbt("NRM", [1, 48], F32)
    NROWB = sbt("NROWB", [1, 1024], F32)
    RT = sbt("RT", [1, 8], F32)
    R_A = sbt("R_A", [1, 2048], F32)
    R_B = sbt("R_B", [1, 2048], F32)
    SINV = sbt("SINV", [1, 1024], F32)
    YN = sbt("YN", [128, 1024], F32)
    SINVB = sbt("SINVB", [128, 1024], F32)
    YODD = sbt("YODD", [128, 1024], BF16)
    EPSC = sbt("EPSC", [128, 1], F32)

    AB0 = psa("AB0", [128, 1024], F32)
    AB1 = psa("AB1", [128, 1024], F32)
    YPS = psa("YPS", [128, 1024], F32)
    PSR = psa("PSR", [128, 512], F32)
    PST = psa("PST", [128, 1024], BF16)

    s_dg = prog.mksem(nc, ctx, "s_dg")
    s_ds = prog.mksem(nc, ctx, "s_ds")
    s_dx = [prog.mksem(nc, ctx, "s_dx0"), prog.mksem(nc, ctx, "s_dx1")]
    s_dw = prog.mksem(nc, ctx, "s_dw")
    s_db = [prog.mksem(nc, ctx, "s_db0"), prog.mksem(nc, ctx, "s_db1")]
    s_dd = [prog.mksem(nc, ctx, "s_dd0"), prog.mksem(nc, ctx, "s_dd1"),
            prog.mksem(nc, ctx, "s_dd2"), prog.mksem(nc, ctx, "s_dd3")]
    s_dxf = prog.mksem(nc, ctx, "s_dxf")
    s_xr = prog.mksem(nc, ctx, "s_xr")
    s_yod = prog.mksem(nc, ctx, "s_yod")
    s_stk = prog.mksem(nc, ctx, "s_stk")
    s_stq = prog.mksem(nc, ctx, "s_stq")
    s_vst = prog.mksem(nc, ctx, "s_vst")
    s_pc = [prog.mksem(nc, ctx, f"s_pc{i}") for i in range(4)]
    s_po = [prog.mksem(nc, ctx, f"s_po{i}") for i in range(4)]
    s_pe = prog.mksem(nc, ctx, "s_pe")
    s_dv = prog.mksem(nc, ctx, "s_dv")
    s_ac = prog.mksem(nc, ctx, "s_ac")
    s_out = prog.mksem(nc, ctx, "s_out")

    A_WSGN = {"q": 0, "k": 8192, "v": 16384}
    A_WABS = 24576
    A_XBF = [32768, 36864]
    A_XLT = 40960
    A_XLF = [45056, 49152]
    A_KST = 53248
    A_QST = 55296
    A_VFM = 57344
    A_VST = 58368
    A_JUNK = 62528
    A_ABSQ = 63552

    B_QH = [0, 2048]
    B_KP = [4096, 8192]
    B_VH = [12288, 14368, 16448, 18528]
    B_P = [20608, 21632, 57344, 58368]
    YSLAB = 24576

    C_WPSGN = 0
    C_WPABS = 8192
    C_JUNK = 16384
    C_ATTN = 40960
    C_XFN = 57344
    C_RB = 61440
    C_RBSQ = 61952
    C_XRES1 = 24576
    C_RB8 = 26624
    C_RBSQ8 = 30720

    D_A = 0
    D_B = 16384
    D_JUNK = 20480
    D_HB = [32768, 36864, 40960, 45056]
    D_FO_S = [49152, 53248]
    D_JK8 = 20480

    F_OUT = [0, 8192]
    F_XRES = 0
    F_R1 = 1024
    F_A2 = 5120
    F_B2 = 7168
    F_B2 = 7168
    F_A4 = 5120
    F_B4 = 5632
    F_SCR = 6144

    bf = lambda c0, n: ABF[:, c0:c0 + n]
    fp = lambda c0, n: AFP[:, c0:c0 + n]

    blocks = []

    def p0_sync(p):
        p.dma(id32[:], id32_d, s_ds)
        p.dma(idbf[:], idbf_d, s_ds)
        for i, w in enumerate("qkvp"):
            p.dma(AFP[0:8, 1024 + i * 128:1024 + (i + 1) * 128], bvec[w], s_ds)
        for j, n in enumerate(("s1", "s2", "sh1", "sh2")):
            p.dma(AFP[0:32, 2048 + j * 128:2048 + (j + 1) * 128], sc_in[n], s_ds)
        p.dma(AFP[0:8, 2560:2688], s3_in, s_ds, label="p0_loads")

    def p0_vector(p):
        p.op("memset", ones_f[:], 1.0)
        p.op("memset", ones_b[:], 1.0)
        p.op("memset", NRM[:], 0.0)
        p.op("memset", EPSC[:], 1e-5)
        p.op("memset", bf(A_VST, 4160), 1.0, inc=(s_dv, 1), label="p0_memsets")

    blocks.append(("p0", {"sync": p0_sync, "vector": p0_vector}))

    def p1_tensor(p):
        p.wait("p0_loads")
        p.wait("p0_memsets")
        for i in range(4):
            p.op("transpose", PSR[:, i * 8:(i + 1) * 8],
                 AFP[0:8, 1024 + i * 128:1024 + (i + 1) * 128], id32[0:8, 0:8],
                 inc=((s_pe, 1) if i == 3 else ()), label=("p1_tr1" if i == 3 else None))
        for j in range(4):
            p.op("transpose", AB0[:, j * 32:(j + 1) * 32],
                 AFP[0:32, 2048 + j * 128:2048 + (j + 1) * 128], id32[0:32, 0:32])
        p.op("transpose", AB0[:, 128:136], AFP[0:8, 2560:2688], id32[0:8, 0:8],
             inc=(s_pe, 1), label="p1_tr2")

    def p1_vector(p):
        p.wait("p1_tr1")
        for i, w in enumerate("qkvp"):
            p.op("tensor_copy", bcol[w][:], PSR[:, i * 8:(i + 1) * 8])
        p.wait("p1_tr2")
        for j, n in enumerate(("s1", "s2", "sh1", "sh2")):
            p.op("tensor_copy", scol[n][:], AB0[:, j * 32:(j + 1) * 32])
        p.op("tensor_copy", s3col[:], AB0[:, 128:136])
        p.op("tensor_scalar_mul", bcol["q"][:], bcol["q"][:], 0.125,
             inc=(s_dv, 1), label="p1_cols")

    blocks.append(("p1", {"tensor": p1_tensor, "vector": p1_vector}))

    a_groups = []
    for nt in range(NT_K):
        for w in ("k", "v", "q"):
            if w == "q" and nt >= NT_Q:
                continue
            for ob in range(8):
                a_groups.append((w, nt, ob))
    gidx = {g: i for i, g in enumerate(a_groups)}
    kq_list = [i for i, g in enumerate(a_groups) if g[0] in ("k", "q")]
    kq_pos = {g: j for j, g in enumerate(kq_list)}
    v_list = [i for i, g in enumerate(a_groups) if g[0] == "v"]
    v_pos = {g: j for j, g in enumerate(v_list)}
    w_lists = {w: [i for i, g in enumerate(a_groups) if g[0] == w] for w in "kvq"}
    w_pos = {w: {g: j for j, g in enumerate(w_lists[w])} for w in "kvq"}

    def acc_of(gi):
        t, h = [(AB0, 0), (AB0, 1), (AB1, 0), (AB1, 1)][gi % 4]
        return t[:, h * 512:(h + 1) * 512]

    def a_gpsimd(p):
        for w in "qkv":
            src = wT[w].rearrange("(cc p) o -> p cc o", p=128)
            p.dma(bf(A_WSGN[w], 8192).rearrange("p (cc o) -> p cc o", cc=8),
                  src, s_dw, label=f"a_w{w}")
        for nt in range(NT_K):
            if nt >= 2:
                p.wait(f"a_xuse_{nt - 2}")
                p.wait(f"a_xdve_{nt - 2}")
            src = x_in[nt * 512:(nt + 1) * 512, :].rearrange("(tc p) c -> p tc c", p=128)
            p.dma(bf(A_XBF[nt % 2], 4096).rearrange("p (tc c) -> p tc c", tc=4),
                  src, s_dx[nt % 2], label=f"a_x_{nt}")
        p.wait_sem(s_dw, s_dw.n)
        for sx in s_dx:
            p.wait_sem(sx, sx.n)

    def a_scalar(p):
        prevw = None
        for w in "qkv":
            p.wait("a_wv")
            if prevw is not None:
                p.wait(f"a_acol_{prevw}")
            sl = bf(A_WSGN[w], 8192)
            ab = bf(A_WABS, 8192)
            p.op("activation", ab, sl, AF.Abs, inc=(s_ac, 1), label=f"a_abs_{w}")
            p.op("activation", sl, sl, AF.Sign, inc=(s_ac, 1), label=f"a_sgn_{w}")
            prevw = w
        def emit_sq(n):
            p.wait(f"a_x_{n}")
            for tc in range(4):
                xt = bf(A_XBF[n % 2] + tc * 1024, 1024)
                if n > 0:
                    p.wait(f"a_var_{n - 1}_{tc}")
                p.op("activation", bf(65536, 1024), xt, AF.Square,
                     accum_out=ln1c[:, tc * 6 + 1:tc * 6 + 2], inc=(s_ac, 1),
                     label=f"a_sq_{n}_{tc}")
                if tc == 3:
                    prog.set_label(f"a_xuse_{n}", s_ac, s_ac.n)

        emit_sq(0)
        for nt in range(NT_K):
            if nt + 1 < NT_K:
                emit_sq(nt + 1)
            for tc in range(4):
                r = tc
                p.wait(f"a_var_{nt}_{tc}")
                p.op("activation", ln1c[:, r * 6 + 4:r * 6 + 5],
                     ln1c[:, r * 6 + 3:r * 6 + 4], AF.Sqrt, bias=EPSC[:, 0:1],
                     inc=(s_ac, 1), label=f"a_sqv_{nt}_{tc}")
            for w in ("k", "q"):
                if w == "q" and nt >= NT_Q:
                    continue
                for ob in range(8):
                    gi = gidx[(w, nt, ob)]
                    jw = w_pos[w][gi]
                    p.wait(f"a_ep_{gi}")
                    if jw >= 2:
                        p.wait(f"a_nrow_{w_lists[w][jw - 2]}")
                    st = bf((A_KST if w == "k" else A_QST) + (jw % 4) * 512, 512)
                    dst = bf((A_JUNK if w == "k" else A_ABSQ) + (jw % 2) * 512, 512)
                    p.op("activation", dst, st, AF.Square if w == "k" else AF.Abs,
                         inc=(s_ac, 1), label=f"a_nsq_{gi}")
        p.wait("a_nrm_done")
        p.op("activation", NRM[0:1, 16:32], NRM[0:1, 16:32], AF.Sqrt,
             inc=(s_ac, 1), label="a_mhsq")

    def a_vector(p):
        for w in "qkv":
            p.wait(f"a_alpha_{w}")
            sc = (1.0 / 1024) * (0.125 if w == "q" else 1.0)
            p.op("tensor_scalar_mul", alpha[w][:], PSR[:, 0:8], sc,
                 inc=(s_dv, 1), label=f"a_acol_{w}")
        def ln1_stats(p, n):
            c = ln1c
            p.wait(f"a_x_{n}")
            for tc in range(4):
                xt = bf(A_XBF[n % 2] + tc * 1024, 1024)
                if n > 0:
                    p.wait(f"a_xln_{n - 1}_{tc}")
                p.op("reduce_sum", c[:, tc * 6:tc * 6 + 1], xt, AX.X)
            for tc in range(4):
                p.op("tensor_scalar_mul", c[:, tc * 6:tc * 6 + 1],
                     c[:, tc * 6:tc * 6 + 1], 1.0 / 1024)
            for tc in range(4):
                p.wait(f"a_sq_{n}_{tc}")
                p.op("tensor_scalar_mul", c[:, tc * 6 + 1:tc * 6 + 2],
                     c[:, tc * 6 + 1:tc * 6 + 2], 1.0 / 1024)
            for tc in range(4):
                p.op("scalar_tensor_tensor", c[:, tc * 6 + 3:tc * 6 + 4],
                     c[:, tc * 6:tc * 6 + 1], -1.0, c[:, tc * 6:tc * 6 + 1],
                     ALU.mult, ALU.mult)
            for tc in range(4):
                p.op("tensor_tensor", c[:, tc * 6 + 3:tc * 6 + 4],
                     c[:, tc * 6 + 1:tc * 6 + 2], c[:, tc * 6 + 3:tc * 6 + 4],
                     ALU.add, inc=(s_dv, 1), label=f"a_var_{n}_{tc}")

        ln1_stats(p, 0)
        for nt in range(NT_K):
            c = ln1c
            for tc in range(4):
                p.wait(f"a_sqv_{nt}_{tc}")
                p.op("reciprocal", c[:, tc * 6 + 2:tc * 6 + 3],
                     c[:, tc * 6 + 4:tc * 6 + 5])
            for tc in range(4):
                p.op("scalar_tensor_tensor", c[:, tc * 6 + 5:tc * 6 + 6],
                     c[:, tc * 6:tc * 6 + 1], -1.0, c[:, tc * 6 + 2:tc * 6 + 3],
                     ALU.mult, ALU.mult, inc=(s_dv, 1), label=f"a_nmr_{nt}_{tc}")
            for tc in range(4):
                xt = bf(A_XBF[nt % 2] + tc * 1024, 1024)
                if nt >= 1:
                    p.wait(f"a_trx_{nt - 1}_{tc}")
                xlt = bf(A_XLT + tc * 1024, 1024)
                p.op("tensor_scalar", xlt, xt, c[:, tc * 6 + 2:tc * 6 + 3],
                     c[:, tc * 6 + 5:tc * 6 + 6], ALU.mult, ALU.add,
                     inc=(s_dv, 1), label=f"a_xln_{nt}_{tc}")
                if tc == 3:
                    prog.set_label(f"a_xdve_{nt}", s_dv, s_dv.n)
            for tc in range(4):
                p.wait(f"a_trx_{nt}_{tc}")
                xlf = bf(A_XLF[nt % 2], 4096)
                p.op("tensor_copy",
                     xlf.rearrange("p (cc t) -> p cc t", cc=8)[:, :, tc * 128:(tc + 1) * 128],
                     PST[:, :].rearrange("p (cc t) -> p cc t", cc=8),
                     inc=(s_dv, 1), label=f"a_xlf_{nt}_{tc}")
            if nt + 1 < NT_K:
                ln1_stats(p, nt + 1)
            for w in ("k", "v", "q"):
                if w == "q" and nt >= NT_Q:
                    continue
                for ob in range(8):
                    gi = gidx[(w, nt, ob)]
                    p.wait(f"a_mm_{gi}")
                    acc = acc_of(gi)
                    if w == "v":
                        jv = v_pos[gi]
                        if jv >= 2:
                            p.wait(f"a_vtr_{v_list[jv - 2]}")
                        slot = A_VFM + (jv % 2) * 512
                        p.op("tensor_copy", bf(slot, 512), acc,
                             inc=(s_dv, 1), label=f"a_ep_{gi}")
                        p.wait(f"a_vtr_{gi}")
                        vst = ABF[:, A_VST + (2 * ob) * 260:A_VST + (2 * ob + 2) * 260]
                        dst = vst.rearrange("p (hh tc j) -> p hh tc j",
                                            hh=2, tc=4)[:, :, :, 0:64]
                        srcv = PST[:, (jv % 2) * 512:(jv % 2) * 512 + 512].rearrange(
                            "p (tc hh j) -> p hh tc j", tc=4, hh=2)
                        p.op("tensor_copy", dst, srcv, inc=(s_dv, 1),
                             label=f"a_vev_{gi}")
                    else:
                        jw = w_pos[w][gi]
                        p.wait(f"a_acol_{w}")
                        if jw >= 4:
                            p.wait(f"a_st_{w_lists[w][jw - 1]}")
                        st = bf((A_KST if w == "k" else A_QST) + (jw % 4) * 512, 512)
                        p.op("tensor_scalar", st, acc, alpha[w][:, ob:ob + 1],
                             bcol[w][:, ob:ob + 1], ALU.mult, ALU.add,
                             inc=(s_dv, 1), label=f"a_ep_{gi}")
                        jkq = kq_pos[gi]
                        if jkq >= 1:
                            pg = kq_list[jkq - 1]
                            pw, pnt, pob = a_groups[pg]
                            p.wait(f"a_nrow_{pg}")
                            base = 16 if pw == "k" else 0
                            h0, h1 = 2 * pob, 2 * pob + 1
                            p.op("tensor_reduce", RT[0:1, 0:1], YPS[0:1, 0:512],
                                 AX.X, ALU.max)
                            p.op("tensor_reduce", RT[0:1, 1:2], YPS[0:1, 512:1024],
                                 AX.X, ALU.max)
                            p.op("tensor_tensor", NRM[0:1, base + h0:base + h0 + 1],
                                 NRM[0:1, base + h0:base + h0 + 1], RT[0:1, 0:1],
                                 ALU.max)
                            p.op("tensor_tensor", NRM[0:1, base + h1:base + h1 + 1],
                                 NRM[0:1, base + h1:base + h1 + 1], RT[0:1, 1:2],
                                 ALU.max, inc=(s_dv, 1), label=f"a_nre_{pg}")
                            prog.set_label(f"a_nmax_{pg}", s_dv, s_dv.n)
        pg = kq_list[-1]
        pw, pnt, pob = a_groups[pg]
        p.wait(f"a_nrow_{pg}")
        base = 16 if pw == "k" else 0
        h0, h1 = 2 * pob, 2 * pob + 1
        p.op("tensor_reduce", RT[0:1, 0:1], YPS[0:1, 0:512], AX.X, ALU.max)
        p.op("tensor_reduce", RT[0:1, 1:2], YPS[0:1, 512:1024], AX.X, ALU.max)
        p.op("tensor_tensor", NRM[0:1, base + h0:base + h0 + 1],
             NRM[0:1, base + h0:base + h0 + 1], RT[0:1, 0:1], ALU.max)
        p.op("tensor_tensor", NRM[0:1, base + h1:base + h1 + 1],
             NRM[0:1, base + h1:base + h1 + 1], RT[0:1, 1:2], ALU.max,
             inc=(s_dv, 1), label=f"a_nre_{pg}")
        p.op("memset", NROWB[0:1, 0:512], 0.0)
        p.op("memset", NROWB[0:1, 512:1024], 0.0, inc=(s_dv, 1),
             label="a_nrm_done")
        p.wait("a_mhsq")
        p.op("tensor_tensor", NRM[0:1, 32:48], NRM[0:1, 0:16], NRM[0:1, 16:32],
             ALU.mult)
        p.op("memset", NROWB[0:1, 0:8], 0.0)
        p.op("memset", NROWB[0:1, 8:16], 0.0)
        pe_ = NRM[0:1, 32:48].rearrange("a (pr two) -> a pr two", two=2)
        p.op("tensor_reduce", RT[0:1, 0:8], pe_, AX.X, ALU.max)
        p.op("memset", NROWB[0:1, 16:24], 0.0)
        p.op("memset", NROWB[0:1, 24:32], 0.0)
        p.op("tensor_scalar_mul", NRM[0:1, 32:40], RT[0:1, 0:8], -1.0)
        p.op("memset", NROWB[0:1, 32:40], 0.0)
        p.op("memset", NROWB[0:1, 40:48], 0.0, inc=(s_dv, 1),
             label="a_mh_rows")
        p.wait("a_mhb")
        p.op("tensor_copy", MHC[:], PSR[:, 16:24], inc=(s_dv, 1), label="a_mhc")

    def a_tensor(p):
        prevw = None
        for w in "qkv":
            p.wait(f"a_abs_{w}")
            if prevw is not None:
                p.wait(f"a_acol_{prevw}")
            ab = bf(A_WABS, 8192)
            for ob in range(8):
                for cc in range(8):
                    last = (ob == 7 and cc == 7)
                    p.op("matmul", PSR[:, ob:ob + 1],
                         ab[:, cc * 1024 + ob * 128:cc * 1024 + (ob + 1) * 128],
                         ones_b[:], start=(cc == 0), stop=(cc == 7),
                         inc=((s_pe, 1) if last else ()),
                         label=(f"a_alpha_{w}" if last else None))
            prevw = w
        pend = [None]

        def flush_aux():
            if pend[0] is None:
                return
            if pend[0][0] == "kq":
                _, gi, w_, _pnt = pend[0]
                jkq = kq_pos[gi]
                p.wait(f"a_nsq_{gi}")
                if jkq >= 1:
                    p.wait(f"a_nre_{kq_list[jkq - 1]}")
                jw = w_pos[w_][gi]
                srcn = bf((A_JUNK if w_ == "k" else A_ABSQ)
                          + (jw % 2) * 512, 512)
                p.op("matmul", YPS[0:1, 0:512], ones_b[0:64, :],
                     srcn[0:64, :], start=True, stop=True)
                p.op("matmul", YPS[0:1, 512:1024], ones_b[64:128, :],
                     srcn[64:128, :], start=True, stop=True,
                     inc=(s_pe, 1), label=f"a_nrow_{gi}")
            else:
                _, gi, vnt = pend[0]
                jv = v_pos[gi]
                p.wait(f"a_ep_{gi}")
                if jv >= 2:
                    p.wait(f"a_vev_{v_list[jv - 2]}")
                if vnt >= 1:
                    p.wait(f"a_vdma_{vnt - 1}")
                vfm = bf(A_VFM + (jv % 2) * 512, 512)
                for tcl in range(4):
                    p.op("transpose",
                         PST[:, (jv % 2) * 512 + tcl * 128:
                             (jv % 2) * 512 + (tcl + 1) * 128],
                         vfm[:, tcl * 128:(tcl + 1) * 128], idbf[:],
                         inc=((s_pe, 1) if tcl == 3 else ()),
                         label=(f"a_vtr_{gi}" if tcl == 3 else None))
            pend[0] = None
        for nt in range(NT_K):
            if pend[0] is not None and pend[0][0] == "v":
                flush_aux()
            for tc in range(4):
                p.wait(f"a_xln_{nt}_{tc}")
                if nt > 0 or tc > 0:
                    pnt, ptc = (nt, tc - 1) if tc > 0 else (nt - 1, 3)
                    p.wait(f"a_xlf_{pnt}_{ptc}")
                if nt > 0 and tc == 0:
                    p.wait(f"a_vev_{gidx[('v', nt - 1, 6)]}")
                    p.wait(f"a_vev_{gidx[('v', nt - 1, 7)]}")
                xlt = bf(A_XLT + tc * 1024, 1024)
                for cc in range(8):
                    p.op("transpose", PST[:, cc * 128:(cc + 1) * 128],
                         xlt[:, cc * 128:(cc + 1) * 128], idbf[:],
                         inc=((s_pe, 1) if cc == 7 else ()),
                         label=(f"a_trx_{nt}_{tc}" if cc == 7 else None))
            p.wait(f"a_xlf_{nt}_3")
            xlf = bf(A_XLF[nt % 2], 4096)
            for w in ("k", "v", "q"):
                if w == "q" and nt >= NT_Q:
                    continue
                p.wait(f"a_sgn_{w}")
                sgn = bf(A_WSGN[w], 8192)
                for ob in range(8):
                    gi = gidx[(w, nt, ob)]
                    if gi >= 4:
                        p.wait(f"a_ep_{gi - 4}")
                    acc = acc_of(gi)
                    for cc in range(8):
                        p.op("matmul", acc,
                             sgn[:, cc * 1024 + ob * 128:cc * 1024 + (ob + 1) * 128],
                             xlf[:, cc * 512:(cc + 1) * 512],
                             start=(cc == 0), stop=(cc == 7),
                             inc=((s_pe, 1) if cc == 7 else ()),
                             label=(f"a_mm_{gi}" if cc == 7 else None))
                    flush_aux()
                    if w in ("k", "q"):
                        pend[0] = ("kq", gi, w, nt)
                    else:
                        pend[0] = ("v", gi, nt)
        flush_aux()
        p.wait("a_mh_rows")
        for pr in range(8):
            p.op("matmul", PSR[:, 16 + pr:17 + pr], ones_f[:],
                 NRM[0:1, 32 + pr:33 + pr], start=True, stop=True,
                 inc=((s_pe, 1) if pr == 7 else ()),
                 label=("a_mhb" if pr == 7 else None))

    def a_sync(p):
        for nt in range(NT_K):
            for w in ("k", "q"):
                if w == "q" and nt >= NT_Q:
                    continue
                for ob in range(8):
                    gi = gidx[(w, nt, ob)]
                    jw = w_pos[w][gi]
                    p.wait(f"a_nsq_{gi}")
                    st = bf((A_KST if w == "k" else A_QST) + (jw % 4) * 512, 512)
                    dst = (k_scr if w == "k" else q_scr)[
                        ob * 128:(ob + 1) * 128, nt * 512:(nt + 1) * 512]
                    p.dma(dst, st, s_stk if w == "k" else s_stq,
                          label=f"a_st_{gi}")
            p.wait(f"a_vev_{gidx[('v', nt, 6)]}")
            p.wait(f"a_vev_{gidx[('v', nt, 7)]}")
            p.dma(v_scr[nt, :, :], bf(A_VST, 4160), s_vst, label=f"a_vdma_{nt}")
        if DEBUG:
            p.wait("a_mhc")
            p.dma(mh_dbg, NRM[0:1, :], s_ds)
        p.wait_sem(s_ds, s_ds.n)

    blocks.append(("a", {"gpsimd": a_gpsimd, "scalar": a_scalar,
                         "vector": a_vector, "tensor": a_tensor,
                         "sync": a_sync}))

    pc_slices = []
    for ob in range(32):
        pc_slices.append(("w1", ob))
        pc_slices.append(("w2", ob))
    for ob in range(32):
        pc_slices.append(("h1", ob))
    for ob in range(32):
        pc_slices.append(("h2", ob))
    for ob in range(8):
        pc_slices.append(("w3", ob))
    PC_SRC = {"w1": (w1T, 8, w1b), "w2": (w2T, 8, w2b),
              "h1": (h1T, 32, h1b), "h2": (h2T, 32, h2b), "w3": (w3T, 32, w3b)}
    PC_STG = [40960, 45056, 49152, 53248]

    if PH_B:
        NCH = 128

        def b_emit_loads(p, pr):
            if pr >= 2:
                p.wait(f"b_s_{(pr - 2) * NCH + NCH - 1}")
                p.wait(f"b_av_{(pr - 2) * NCH + NCH - 1}")
            p.dma(bf(B_QH[pr % 2], 2048), q_scr[pr * 128:(pr + 1) * 128, :],
                  s_db[pr % 2])
            p.dma(bf(B_KP[pr % 2], 4096), k_scr[pr * 128:(pr + 1) * 128, :],
                  s_db[pr % 2])
            for hh in range(2):
                h = 2 * pr + hh
                srcv = v_scr.rearrange("nt p (h x) -> p nt h x", h=16)[
                    :, :, h, :]
                p.dma(bf(B_VH[(pr % 2) * 2 + hh], 2080).rearrange(
                    "p (nt x) -> p nt x", nt=8), srcv, s_db[pr % 2],
                    label=(f"b_dma_{pr}" if hh == 1 else None))

        def b_gpsimd(p):
            npc = len(pc_slices)
            for ipc in range(npc):
                w, ob = pc_slices[ipc]
                wt_, ncc, dst = PC_SRC[w]
                if ipc >= 4:
                    p.wait(f"b_pco_{ipc - 4}")
                srcw = wt_[:, ob * 128:(ob + 1) * 128].rearrange(
                    "(cc p) o -> p cc o", p=128)
                p.dma(bf(PC_STG[ipc % 4], ncc * 128).rearrange(
                    "p (cc o) -> p cc o", cc=ncc), srcw, s_pc[ipc % 4],
                    label=f"b_pcc_{ipc}")
            for sx in s_pc:
                p.wait_sem(sx, sx.n)

        def b_emit_s(p, pr, ci):
            qt, kc = divmod(ci, 32)
            g = pr * NCH + ci
            if g >= 2:
                p.wait(f"b_e_{g - 2}")
            if ci == 0:
                p.wait(f"b_dma_{pr}")
            qh = bf(B_QH[pr % 2], 2048)
            kp = bf(B_KP[pr % 2], 4096)
            ps = AB0 if g % 2 == 0 else AB1
            p.op("matmul", ps[:, 0:512], kp[0:64, kc * 128:(kc + 1) * 128],
                 qh[0:64, qt * 512:(qt + 1) * 512], start=True, stop=True,
                 tile_position=(0, 0))
            p.op("matmul", ps[:, 512:1024], kp[64:128, kc * 128:(kc + 1) * 128],
                 qh[64:128, qt * 512:(qt + 1) * 512], start=True, stop=True,
                 tile_position=(64, 0), inc=(s_pe, 1), label=f"b_s_{g}")

        def b_tensor(p):
            p.wait("a_mhc")
            for pr in range(8):
                if pr == 0:
                    b_emit_s(p, pr, 0)
                vA = bf(B_VH[(pr % 2) * 2], 2080)
                vB = bf(B_VH[(pr % 2) * 2 + 1], 2080)
                for ci in range(NCH):
                    qt, kc = divmod(ci, 32)
                    g = pr * NCH + ci
                    if ci + 1 < NCH:
                        b_emit_s(p, pr, ci + 1)
                    elif pr + 1 < 8:
                        b_emit_s(p, pr + 1, 0)
                    p.wait(f"b_e_{g}")
                    if kc == 0:
                        if qt > 0:
                            p.wait(f"b_ynn_{pr}_{qt - 1}_1")
                        elif pr > 0:
                            p.wait(f"b_ynn_{pr - 1}_3_1")
                    pt = bf(B_P[g % 4], 1024)
                    p.op("matmul", YPS[0:65, 0:512], vA[:, kc * 65:(kc + 1) * 65],
                         pt[:, 0:512], start=(kc == 0), stop=(kc == 31))
                    p.op("matmul", YPS[0:65, 512:1024], vB[:, kc * 65:(kc + 1) * 65],
                         pt[:, 512:1024], start=(kc == 0), stop=(kc == 31),
                         inc=(s_pe, 1), label=f"b_av_{g}")
                    if kc == 31:
                        p.wait(f"b_rc_{pr}_{qt}_0")
                        if pr > 0 or qt > 0:
                            pt = pr * 4 + qt - 1
                            p.wait(f"b_cpa_{pt // 4}_{pt % 4}")
                        p.op("matmul", PSR[0:64, :], ones_f[0:1, 0:64],
                             SINV[0:1, 0:512], start=True, stop=True,
                             inc=(s_pe, 1), label=f"b_bc_{pr}_{qt}_0")
                        p.op("matmul", PSR[64:128, :], ones_f[0:1, 0:64],
                             SINV[0:1, 512:1024], start=True, stop=True,
                             tile_position=(0, 64),
                             inc=(s_pe, 1), label=f"b_bc_{pr}_{qt}_1")


        def b_scalar(p):
            b_emit_loads(p, 0)
            b_emit_loads(p, 1)
            for pr in range(8):
                for ci in range(NCH):
                    if ci == 3 and 2 <= pr + 1 < 8:
                        b_emit_loads(p, pr + 1)
                    qt, kc = divmod(ci, 32)
                    g = pr * NCH + ci
                    p.wait(f"b_s_{g}")
                    if g >= 4:
                        p.wait(f"b_av_{g - 4}")
                    ps = AB0 if g % 2 == 0 else AB1
                    p.op("activation", bf(B_P[g % 4], 1024), ps[:, :], AF.Exp,
                         bias=MHC[:, pr:pr + 1], inc=(s_ac, 1), label=f"b_e_{g}")
            for sx in s_db:
                p.wait_sem(sx, sx.n)


        def b_vector(p):
            for pr in range(8):
                for qt in range(4):
                    g_last = pr * NCH + qt * 32 + 31
                    p.wait(f"b_av_{g_last}")
                    p.op("reciprocal", SINV[0:1, 0:512], YPS[64:65, 0:512])
                    p.op("reciprocal", SINV[0:1, 512:1024], YPS[64:65, 512:1024],
                         inc=(s_dv, 1), label=f"b_rc_{pr}_{qt}_0")
                    prog.set_label(f"b_rc_{pr}_{qt}_1", s_dv, s_dv.n)
                    p.wait(f"b_bc_{pr}_{qt}_1")
                    p.op("tensor_copy", SINVB[0:64, 0:512], PSR[0:64, :])
                    p.op("tensor_copy", SINVB[0:64, 512:1024], PSR[64:128, :],
                         inc=(s_dv, 1), label=f"b_cpa_{pr}_{qt}")
                    p.op("memset", RT[0:1, 0:8], 0.0)
                    p.op("tensor_tensor", YN[0:64, 0:512], SINVB[0:64, 0:512],
                         YPS[0:64, 0:512], ALU.mult)
                    p.op("tensor_tensor", YN[0:64, 512:1024],
                         SINVB[0:64, 512:1024], YPS[0:64, 512:1024], ALU.mult,
                         inc=(s_dv, 1), label=f"b_ynn_{pr}_{qt}_1")
                    prog.set_label(f"b_ynn_{pr}_{qt}_0", s_dv, s_dv.n)
                    t = pr * 4 + qt
                    if t >= 2:
                        tp, tq = divmod(t - 2, 4)
                        p.wait(f"b_yod_{tp}_{tq}")
                    dsty = ABF[0:64, YSLAB + pr * 2048 + qt * 512:
                               YSLAB + pr * 2048 + (qt + 1) * 512]
                    p.op("tensor_scalar", dsty, YN[0:64, 0:512],
                         alpha["v"][0:64, pr:pr + 1], bcol["v"][0:64, pr:pr + 1],
                         ALU.mult, ALU.add,
                         inc=(s_dv, 1), label=f"b_ysl_{pr}_{qt}_0")
                    p.op("tensor_scalar",
                         YODD[0:64, (t % 2) * 512:(t % 2 + 1) * 512],
                         YN[0:64, 512:1024],
                         alpha["v"][64:128, pr:pr + 1], bcol["v"][64:128, pr:pr + 1],
                         ALU.mult, ALU.add,
                         inc=(s_dv, 1), label=f"b_ysl_{pr}_{qt}_1")


        def b_sync(p):
            npc = len(pc_slices)
            ipc = 0

            def emit_pco(n):
                nonlocal ipc
                for _ in range(n):
                    if ipc >= npc:
                        return
                    w, ob = pc_slices[ipc]
                    wt_, ncc, dst = PC_SRC[w]
                    p.wait(f"b_pcc_{ipc}")
                    p.dma(dst[ob], bf(PC_STG[ipc % 4], ncc * 128), s_po[ipc % 4],
                          label=f"b_pco_{ipc}")
                    ipc += 1
            for pr in range(8):
                for qt in range(4):
                    t = pr * 4 + qt
                    for k6 in range(6):
                        if ipc < npc:
                            g = pr * NCH + qt * 32 + 5 * k6 + 1
                            p.wait(f"b_e_{g}")
                            emit_pco(1)
                    p.wait(f"b_ysl_{pr}_{qt}_1")
                    p.dma(ABF[64:128, YSLAB + pr * 2048 + qt * 512:
                              YSLAB + pr * 2048 + (qt + 1) * 512],
                          YODD[0:64, (t % 2) * 512:(t % 2 + 1) * 512],
                          s_yod, label=f"b_yod_{pr}_{qt}")
            emit_pco(npc)
            if DEBUG:
                p.wait("b_ysl_7_3_0")
                p.wait("b_yod_7_3")
                p.dma(y_dbg, ABF[:, YSLAB:YSLAB + 16384], s_ds)
            p.wait_sem(s_ds, s_ds.n)
            p.wait_sem(s_yod, s_yod.n)
            for sx in s_po:
                p.wait_sem(sx, sx.n)

        blocks.append(("b", {"gpsimd": b_gpsimd, "tensor": b_tensor,
                             "scalar": b_scalar, "vector": b_vector,
                             "sync": b_sync}))

    if PH_C:
        def c_gpsimd(p):
            src = wT["p"].rearrange("(cc p) o -> p cc o", p=128)
            p.dma(bf(C_WPSGN, 8192).rearrange("p (cc o) -> p cc o", cc=8),
                  src, s_dw, label="c_wp")
            p.wait_sem(s_dw, s_dw.n)

        def c_scalar(p):
            p.wait("c_wp")
            p.op("activation", bf(C_WPABS, 8192), bf(C_WPSGN, 8192), AF.Abs,
                 inc=(s_ac, 1), label="c_abs")
            p.op("activation", bf(C_WPSGN, 8192), bf(C_WPSGN, 8192), AF.Sign,
                 inc=(s_ac, 1), label="c_sgn")

            for hp in range(2):
                for ob in range(8):
                    p.wait(f"c_ep_{ob * 4 + 3}")
                    if not (hp == 0 and ob < 2):
                        pj = (hp, ob - 2) if ob >= 2 else (hp - 1, 6 + ob)
                        p.wait(f"c_st2_{pj[0]}_{pj[1]}")
                    p.op("activation", bf(C_JUNK + (ob % 2) * 1024, 1024),
                         bf(C_ATTN + ob * 2048 + hp * 1024, 1024), AF.Square,
                         inc=(s_ac, 1), label=f"c_sqs_{hp}_{ob}")
            p.wait("c_var2")
            p.op("activation", R_B[0:1, :], R_B[0:1, :], AF.Sqrt, bias=EPSC[0:1, 0:1],
                 inc=(s_ac, 1), label="c_sq2")
            def emit_sq3(n):
                p.wait(f"c_var3_{n}")
                p.op("activation", R_B[0:1, n * 512:(n + 1) * 512],
                     R_B[0:1, n * 512:(n + 1) * 512], AF.Sqrt,
                     bias=EPSC[0:1, 0:1], inc=(s_ac, 1), label=f"c_sq3_{n}")
            for nt in range(4):
                p.wait(f"c_r1_{nt}")
                if nt > 0:
                    p.wait(f"c_st3_{nt - 1}_{7}")
                for cc in range(8):
                    p.op("activation", bf(C_RB8 + cc * 512, 512),
                         AFP[:, F_R1 + cc * 512:F_R1 + (cc + 1) * 512], AF.Copy,
                         inc=(s_ac, 1))
                if nt > 0:
                    emit_sq3(nt - 1)
                for cc in range(8):
                    p.op("activation", bf(C_RBSQ8 + cc * 512, 512),
                         bf(C_RB8 + cc * 512, 512), AF.Square,
                         inc=(s_ac, 1), label=f"c_rbsq_{nt}_{cc}")
            emit_sq3(3)

        def c_tensor(p):
            p.wait("c_abs")
            ab = bf(C_WPABS, 8192)
            for ob in range(8):
                for cc in range(8):
                    last = (ob == 7 and cc == 7)
                    p.op("matmul", PSR[:, ob:ob + 1],
                         ab[:, cc * 1024 + ob * 128:cc * 1024 + (ob + 1) * 128],
                         ones_b[:], start=(cc == 0), stop=(cc == 7),
                         inc=((s_pe, 1) if last else ()),
                         label=("c_alpha" if last else None))
            p.wait("c_sgn")
            sgn = bf(C_WPSGN, 8192)
            ysl = ABF[:, YSLAB:YSLAB + 16384]
            for ob in range(8):
                for nt in range(4):
                    g = ob * 4 + nt
                    if g >= 3:
                        p.wait(f"c_ep_{g - 3}")
                    acc = [AB0[:, 0:512], AB0[:, 512:1024], AB1[:, 0:512]][g % 3]
                    for cc in range(8):
                        p.op("matmul", acc,
                             sgn[:, cc * 1024 + ob * 128:cc * 1024 + (ob + 1) * 128],
                             ysl[:, cc * 2048 + nt * 512:cc * 2048 + (nt + 1) * 512],
                             start=(cc == 0), stop=(cc == 7),
                             inc=((s_pe, 1) if cc == 7 else ()),
                             label=(f"c_mm_{g}" if cc == 7 else None))
            p.wait(f"c_ep_{31}")
            for hp in range(2):
                if hp > 0:
                    p.wait(f"c_sre2_{hp - 1}")
                for ob in range(8):
                    p.wait(f"c_sqs_{hp}_{ob}")
                    for ntl in range(2):
                        nt = hp * 2 + ntl
                        p.op("matmul", YPS[0:1, ntl * 512:(ntl + 1) * 512],
                             ones_b[:], bf(C_ATTN + ob * 2048 + nt * 512, 512),
                             start=(ob == 0), stop=(ob == 7))
                        p.op("matmul", AB0[0:1, ntl * 512:(ntl + 1) * 512],
                             ones_b[:],
                             bf(C_JUNK + (ob % 2) * 1024 + ntl * 512, 512),
                             start=(ob == 0), stop=(ob == 7),
                             inc=((s_pe, 1) if ntl == 1 else ()),
                             label=(f"c_st2_{hp}_{ob}" if ntl == 1 else None))
                prog.set_label(f"c_str2_{hp}", s_pe, s_pe.n)
            for nt in range(4):
                p.wait("c_nmr2")
                if nt > 0:
                    p.wait(f"c_b2e_{nt - 1}")
                p.op("matmul", PSR[:, 0:512], ones_f[:],
                     R_B[0:1, nt * 512:(nt + 1) * 512], start=True, stop=True,
                     inc=(s_pe, 1), label=f"c_ab2_{nt}")
                p.wait(f"c_a2e_{nt}")
                p.op("matmul", PSR[:, 0:512], ones_f[:],
                     R_A[0:1, nt * 512:(nt + 1) * 512], start=True, stop=True,
                     inc=(s_pe, 1), label=f"c_bb2_{nt}")
            def emit_bc3(n):
                p.wait(f"c_nmr3_{n}")
                if n > 0:
                    p.wait(f"c_b3e_{n - 1}")
                p.op("matmul", PSR[:, 0:512], ones_f[:],
                     R_B[0:1, n * 512:(n + 1) * 512], start=True, stop=True,
                     inc=(s_pe, 1), label=f"c_ab3_{n}")
                p.wait(f"c_a3e_{n}")
                p.op("matmul", PSR[:, 0:512], ones_f[:],
                     R_A[0:1, n * 512:(n + 1) * 512], start=True, stop=True,
                     inc=(s_pe, 1), label=f"c_bb3_{n}")
            xres_slots = [fp(F_XRES, 1024),
                          ABF[:, C_XRES1:C_XRES1 + 2048].bitcast(F32)]
            for nt in range(4):
                for tcl in range(4):
                    tc = nt * 4 + tcl
                    p.wait(f"c_xr_{tc}")
                    if tc >= 2:
                        p.wait(f"c_xfadd_{tc - 2}")
                    else:
                        p.wait("c_sre2_1")
                    if tcl == 0 and nt > 0:
                        p.wait(f"c_sre3_{nt - 1}")
                    xr = xres_slots[tc % 2]
                    tb = YPS if tc % 2 == 0 else AB0
                    for cc in range(8):
                        p.op("transpose", tb[:, cc * 128:(cc + 1) * 128],
                             xr[:, cc * 128:(cc + 1) * 128], id32[:],
                             inc=((s_pe, 1) if cc == 7 else ()),
                             label=(f"c_tr_{tc}" if cc == 7 else None))
                p.wait(f"c_xfadd_{nt * 4 + 3}")
                if nt > 0:
                    p.wait(f"c_sre3_{nt - 1}")
                for cc in range(8):
                    p.wait(f"c_rbsq_{nt}_{cc}")
                    p.op("matmul", YPS[0:1, 0:512], ones_b[:],
                         bf(C_RB8 + cc * 512, 512),
                         start=(cc == 0), stop=(cc == 7))
                    p.op("matmul", YPS[0:1, 512:1024], ones_b[:],
                         bf(C_RBSQ8 + cc * 512, 512),
                         start=(cc == 0), stop=(cc == 7),
                         inc=(s_pe, 1), label=f"c_st3_{nt}_{cc}")
                prog.set_label(f"c_str3_{nt}", s_pe, s_pe.n)
                if nt > 0:
                    emit_bc3(nt - 1)
            emit_bc3(3)

        def c_vector(p):
            p.wait("c_alpha")
            p.op("tensor_scalar_mul", alpha["p"][:], PSR[:, 0:8], 1.0 / 1024,
                 inc=(s_dv, 1), label="c_acol")
            for ob in range(8):
                for nt in range(4):
                    g = ob * 4 + nt
                    p.wait(f"c_mm_{g}")
                    acc = [AB0[:, 0:512], AB0[:, 512:1024], AB1[:, 0:512]][g % 3]
                    p.op("tensor_scalar", bf(C_ATTN + ob * 2048 + nt * 512, 512),
                         acc, alpha["p"][:, ob:ob + 1], bcol["p"][:, ob:ob + 1],
                         ALU.mult, ALU.add, inc=(s_dv, 1), label=f"c_ep_{g}")
            for hp in range(2):
                p.wait(f"c_str2_{hp}")
                for ntl in range(2):
                    nt = hp * 2 + ntl
                    p.op("tensor_scalar_mul", R_A[0:1, nt * 512:(nt + 1) * 512],
                         YPS[0:1, ntl * 512:(ntl + 1) * 512], 1.0 / 1024)
                    p.op("tensor_scalar_mul", R_B[0:1, nt * 512:(nt + 1) * 512],
                         AB0[0:1, ntl * 512:(ntl + 1) * 512], 1.0 / 1024,
                         inc=((s_dv, 1) if ntl == 1 else ()),
                         label=(f"c_sre2_{hp}" if ntl == 1 else None))
            for h2 in range(2):
                for j in (0, 1):
                    jj = h2 * 2 + j
                    p.op("tensor_tensor", NROWB[0:1, j * 512:(j + 1) * 512],
                         R_A[0:1, jj * 512:(jj + 1) * 512],
                         R_A[0:1, jj * 512:(jj + 1) * 512], ALU.mult)
                for j in (0, 1):
                    jj = h2 * 2 + j
                    p.op("tensor_tensor", R_B[0:1, jj * 512:(jj + 1) * 512],
                         R_B[0:1, jj * 512:(jj + 1) * 512],
                         NROWB[0:1, j * 512:(j + 1) * 512], ALU.subtract,
                         inc=((s_dv, 1) if jj == 3 else ()),
                         label=("c_var2" if jj == 3 else None))
            p.wait("c_sq2")
            p.op("reciprocal", R_B[0:1, :], R_B[0:1, :])
            p.op("memset", NROWB[0:1, :], 0.0)
            p.op("scalar_tensor_tensor", R_A[0:1, :], R_A[0:1, :], -1.0,
                 R_B[0:1, :], ALU.mult, ALU.mult)
            p.op("memset", NROWB[0:1, 0:512], 0.0)
            p.op("memset", NROWB[0:1, 512:1024], 0.0, inc=(s_dv, 1),
                 label="c_nmr2")
            for nt in range(4):
                p.wait(f"c_ab2_{nt}")
                p.op("tensor_copy", AFP[:, F_A2 + nt * 512:F_A2 + (nt + 1) * 512],
                     PSR[:, 0:512], inc=(s_dv, 1), label=f"c_a2e_{nt}")
                p.wait(f"c_bb2_{nt}")
                p.op("tensor_copy", AFP[:, F_B2 + nt * 512:F_B2 + (nt + 1) * 512],
                     PSR[:, 0:512], inc=(s_dv, 1), label=f"c_b2e_{nt}")
            def ln3_fin(n):
                p.wait(f"c_sq3_{n}")
                p.op("reciprocal", R_B[0:1, n * 512:(n + 1) * 512],
                     R_B[0:1, n * 512:(n + 1) * 512])
                p.op("memset", RT[0:1, 0:8], 0.0)
                p.op("scalar_tensor_tensor", R_A[0:1, n * 512:(n + 1) * 512],
                     R_A[0:1, n * 512:(n + 1) * 512], -1.0,
                     R_B[0:1, n * 512:(n + 1) * 512], ALU.mult, ALU.mult,
                     inc=(s_dv, 1), label=f"c_nmr3_{n}")
                p.wait(f"c_ab3_{n}")
                p.op("tensor_copy", AFP[:, F_A2 + n * 512:F_A2 + (n + 1) * 512],
                     PSR[:, 0:512], inc=(s_dv, 1), label=f"c_a3e_{n}")
                p.wait(f"c_bb3_{n}")
                p.op("tensor_copy", AFP[:, F_B2 + n * 512:F_B2 + (n + 1) * 512],
                     PSR[:, 0:512], inc=(s_dv, 1), label=f"c_b3e_{n}")
            for nt in range(4):
                if nt > 0:
                    p.wait(f"c_r1d_{nt - 1}")
                for cc in range(8):
                    at = bf(C_ATTN + cc * 2048 + nt * 512, 512)
                    r1c = AFP[:, F_R1 + cc * 512:F_R1 + (cc + 1) * 512]
                    p.op("tensor_tensor", r1c, at,
                         AFP[:, F_A2 + nt * 512:F_A2 + (nt + 1) * 512], ALU.mult)
                for cc in range(8):
                    r1c = AFP[:, F_R1 + cc * 512:F_R1 + (cc + 1) * 512]
                    p.op("tensor_tensor", r1c, r1c,
                         AFP[:, F_B2 + nt * 512:F_B2 + (nt + 1) * 512], ALU.add,
                         inc=((s_dv, 1) if cc == 7 else ()),
                         label=(f"c_pre_{nt}" if cc == 7 else None))
                for tcl in range(4):
                    tc = nt * 4 + tcl
                    p.wait(f"c_tr_{tc}")
                    tb = YPS if tc % 2 == 0 else AB0
                    for cc in range(8):
                        r1b = AFP[:, F_R1 + cc * 512 + tcl * 128:
                                  F_R1 + cc * 512 + (tcl + 1) * 128]
                        p.op("tensor_tensor", r1b, r1b,
                             tb[:, cc * 128:(cc + 1) * 128], ALU.add,
                             inc=((s_dv, 1) if cc == 7 else ()),
                             label=(f"c_xfadd_{tc}" if cc == 7 else None))
                prog.set_label(f"c_r1_{nt}", s_dv, s_dv.n)
                p.wait(f"c_str3_{nt}")
                p.op("tensor_scalar_mul", R_A[0:1, nt * 512:(nt + 1) * 512],
                     YPS[0:1, 0:512], 1.0 / 1024)
                p.op("tensor_scalar_mul", R_B[0:1, nt * 512:(nt + 1) * 512],
                     YPS[0:1, 512:1024], 1.0 / 1024,
                     inc=(s_dv, 1), label=f"c_sre3_{nt}")
                nsl = NROWB[0:1, (nt % 2) * 512:(nt % 2 + 1) * 512]
                p.op("tensor_tensor", nsl, R_A[0:1, nt * 512:(nt + 1) * 512],
                     R_A[0:1, nt * 512:(nt + 1) * 512], ALU.mult)
                p.op("memset", RT[0:1, 0:8], 0.0)
                p.op("tensor_tensor", R_B[0:1, nt * 512:(nt + 1) * 512],
                     R_B[0:1, nt * 512:(nt + 1) * 512], nsl, ALU.subtract,
                     inc=(s_dv, 1), label=f"c_var3_{nt}")
                if nt > 0:
                    ln3_fin(nt - 1)
            ln3_fin(3)
            for nt in range(4):
                p.wait(f"c_r1l_{nt}")
                for cc in range(8):
                    r1c = AFP[:, F_R1 + cc * 512:F_R1 + (cc + 1) * 512]
                    xfc = bf(C_XFN + cc * 512, 512)
                    p.op("tensor_tensor", xfc, r1c,
                         AFP[:, F_A2 + nt * 512:F_A2 + (nt + 1) * 512], ALU.mult)
                for cc in range(8):
                    xfc = bf(C_XFN + cc * 512, 512)
                    p.op("tensor_tensor", xfc, xfc,
                         AFP[:, F_B2 + nt * 512:F_B2 + (nt + 1) * 512], ALU.add,
                         inc=((s_dv, 1) if cc == 7 else ()),
                         label=(f"c_xfn_{nt}" if cc == 7 else None))

        def c_sync(p):
            for nt in range(4):
                for tcl in range(4):
                    tc = nt * 4 + tcl
                    if tc == 1:
                        p.wait("c_mm_31")
                    if tc >= 2:
                        p.wait(f"c_tr_{tc - 2}")
                    dst = [fp(F_XRES, 1024),
                           ABF[:, C_XRES1:C_XRES1 + 2048].bitcast(F32)][tc % 2]
                    p.dma(dst, x_in[tc * 128:(tc + 1) * 128, :], s_xr,
                          label=f"c_xr_{tc}")
                p.wait(f"c_r1_{nt}")
                p.wait(f"c_st3_{nt}_{7}")
                p.dma(r1_scr.rearrange("cc p t -> p cc t")[:, :, nt * 512:(nt + 1) * 512],
                      fp(F_R1, 4096).rearrange("p (cc t) -> p cc t", cc=8),
                      s_ds, label=f"c_r1d_{nt}")
            for nt in range(4):
                p.wait("c_nmr3_3")
                p.wait(f"c_r1d_{3}")
                if nt > 0:
                    p.wait(f"c_xfn_{nt - 1}")
                p.dma(fp(F_R1, 4096).rearrange("p (cc t) -> p cc t", cc=8),
                      r1_scr.rearrange("cc p t -> p cc t")[:, :, nt * 512:(nt + 1) * 512],
                      s_ds, label=f"c_r1l_{nt}")
                p.wait(f"c_xfn_{nt}")
                p.dma(xffn_scr.rearrange("cc p t -> p cc t")[:, :, nt * 512:(nt + 1) * 512],
                      bf(C_XFN, 4096).rearrange("p (cc t) -> p cc t", cc=8),
                      s_ds, label=f"c_xfd_{nt}")
            p.wait_sem(s_ds, s_ds.n)
            p.wait_sem(s_xr, s_xr.n)

        blocks.append(("c", {"gpsimd": c_gpsimd, "scalar": c_scalar,
                             "tensor": c_tensor, "vector": c_vector,
                             "sync": c_sync}))

    if PH_D:
        NTT = 4
        def wd_seq(tt):
            seq = [("xf", tt, 0)]
            for ob in range(HOB):
                for wi in range(2):
                    seq.append(("w12", tt, ob * 2 + wi))
            for li in range(2):
                for ob in range(HOB):
                    seq.append(("h", tt, li * 32 + ob))
            for ob in range(8):
                seq.append(("w3", tt, ob))
            return seq

        full_seq = []
        for tt in range(NTT):
            full_seq += wd_seq(tt)

        def consumer_label(item):
            kind, tt, j = item
            if kind == "xf":
                return f"d_mm12_{tt}_{1}"
            if kind == "w12":
                return f"d_mm12_{tt}_{j}"
            if kind == "h":
                li, ob = divmod(j, 32)
                return f"d_mmh_{tt}_{li}_{ob}"
            return f"d_mmw3_{tt}_{j}"

        hb_of = {}
        gate_of = {}
        nxf_list = [it for it in full_seq if it[0] != "xf"]
        npos = {}
        for j, it in enumerate(nxf_list):
            npos[it] = j
            hb_of[it] = j % 4
        for idx, item in enumerate(full_seq):
            if item[0] == "xf":
                gate_of[idx] = None
            else:
                j = npos[item]
                gate_of[idx] = nxf_list[j - 4] if j >= 4 else None

        def d_emit_wd(p, q, idx):
            kind, tt, j = full_seq[idx]
            if gate_of[idx] is not None:
                p.wait(consumer_label(gate_of[idx]))
            if kind == "xf":
                if tt > 0:
                    p.wait(f"d_mmh_{tt - 1}_1_31")
                p.dma(ABF[:, D_B:D_B + 4096].rearrange("p (cc t) -> p cc t", cc=8),
                      xffn_scr.rearrange("cc p t -> p cc t")[
                          :, :, tt * 512:(tt + 1) * 512],
                      s_dxf, label=f"d_xf_{tt}")
                return
            jn = npos[full_seq[idx]]
            sem = s_dd[(jn % 2) * 2 + ((jn // 2) % 2)]
            if kind == "w12":
                ob, wi = divmod(j, 2)
                srcb = (w1b if wi == 0 else w2b)[ob]
                p.dma(bf(D_HB[hb_of[full_seq[idx]]], 1024), srcb, sem,
                      label=f"d_wd_{tt}_w12_{j}")
            elif kind == "h":
                li, ob = divmod(j, 32)
                srcb = (h1b if li == 0 else h2b)[ob]
                p.dma(bf(D_HB[hb_of[full_seq[idx]]], 4096), srcb, sem,
                      label=f"d_wd_{tt}_h_{j}")
            else:
                p.dma(bf(D_HB[hb_of[full_seq[idx]]], 4096), w3b[j], sem,
                      label=f"d_wd_{tt}_w3_{j}")

        def q_of(idx):
            it = full_seq[idx]
            if it[0] == "xf":
                return 0
            return npos[it] % 2

        def d_gpsimd(p):
            for idx in range(len(full_seq)):
                if q_of(idx) == 1:
                    d_emit_wd(p, 1, idx)
            p.wait_sem(s_dd[2], s_dd[2].n)
            p.wait_sem(s_dd[3], s_dd[3].n)

        def d_tensor(p):
            for tt in range(NTT):
                p.wait(f"d_xf_{tt}")
                xf = ABF[:, D_B:D_B + 4096]
                for ob in range(HOB):
                    for wi in range(2):
                        g = ob * 2 + wi
                        p.wait(f"d_wd_{tt}_w12_{g}")
                        if g >= 4:
                            pg = g - 4
                            if pg % 2 == 0:
                                p.wait(f"d_t1_{tt}_{pg // 2}")
                            else:
                                p.wait(f"d_ep2_{tt}_{pg // 2}")
                        acc = acc_of(g)
                        wsl = bf(D_HB[hb_of[("w12", tt, g)]], 1024)
                        for cc in range(8):
                            p.op("matmul", acc, wsl[:, cc * 128:(cc + 1) * 128],
                                 xf[:, cc * 512:(cc + 1) * 512],
                                 start=(cc == 0), stop=(cc == 7),
                                 inc=((s_pe, 1) if cc == 7 else ()),
                                 label=(f"d_mm12_{tt}_{g}" if cc == 7 else None))
                for li in range(2):
                    src0 = D_A if li == 0 else D_B
                    for ob in range(HOB):
                        g = li * 32 + ob
                        p.wait(f"d_wd_{tt}_h_{g}")
                        if li == 0 and ob == 0:
                            p.wait(f"d_a_{tt}_{HOB - 1}")
                        if li == 1 and ob == 0:
                            p.wait(f"d_geldone_{tt}_0")
                        gg = ob % 4
                        if ob >= 4 or li == 1:
                            pli, pob = (li, ob - 4) if ob >= 4 else (li - 1, 28 + ob)
                            p.wait(f"d_eph_{tt}_{pli}_{pob}")
                        acc = acc_of(gg)
                        wsl = bf(D_HB[hb_of[("h", tt, g)]], 4096)
                        for cc in range(32):
                            p.op("matmul", acc, wsl[:, cc * 128:(cc + 1) * 128],
                                 ABF[:, src0 + cc * 512:src0 + (cc + 1) * 512],
                                 start=(cc == 0), stop=(cc == 31),
                                 inc=((s_pe, 1) if cc == 31 else ()),
                                 label=(f"d_mmh_{tt}_{li}_{ob}" if cc == 31 else None))
                for ob in range(8):
                    p.wait(f"d_wd_{tt}_w3_{ob}")
                    if ob == 0:
                        p.wait(f"d_geldone_{tt}_1")
                    if ob >= 4:
                        p.wait(f"d_epw3_{tt}_{ob - 4}")
                    acc = acc_of(ob % 4)
                    wsl = bf(D_HB[hb_of[("w3", tt, ob)]], 4096)
                    for cc in range(32):
                        p.op("matmul", acc, wsl[:, cc * 128:(cc + 1) * 128],
                             ABF[:, D_A + cc * 512:D_A + (cc + 1) * 512],
                             start=(cc == 0), stop=(cc == 31),
                             inc=((s_pe, 1) if cc == 31 else ()),
                             label=(f"d_mmw3_{tt}_{ob}" if cc == 31 else None))
                if tt > 0:
                    p.wait(f"d_oev_{tt - 1}_3")
                for ob in range(8):
                    p.wait(f"d_fsq_{tt}_{ob}")
                    p.op("matmul", YPS[0:1, 0:512], ones_b[:],
                         bf(D_FO_S[tt % 2] + ob * 512, 512),
                         start=(ob == 0), stop=(ob == 7))
                    p.op("matmul", YPS[0:1, 512:1024], ones_b[:],
                         bf(D_JK8 + ob * 512, 512), start=(ob == 0), stop=(ob == 7),
                         inc=(s_pe, 1), label=f"d_st4_{tt}_{ob}")
                prog.set_label(f"d_str4_{tt}", s_pe, s_pe.n)
                p.wait(f"d_nmr4_{tt}")
                p.op("matmul", PSR[:, 0:512], ones_f[:], R_B[0:1, 0:512],
                     start=True, stop=True, inc=(s_pe, 1), label=f"d_ab4_{tt}")
                p.wait(f"d_a4e_{tt}")
                p.op("matmul", PSR[:, 0:512], ones_f[:], R_A[0:1, 0:512],
                     start=True, stop=True, inc=(s_pe, 1), label=f"d_bb4_{tt}")
                for tcl in range(4):
                    p.wait(f"d_of_{tt}")
                    if tcl > 0:
                        p.wait(f"d_oev_{tt}_{tcl - 1}")
                    elif tt > 0:
                        p.wait(f"d_oev_{tt - 1}_3")
                    for cc in range(8):
                        p.op("transpose", YPS[:, cc * 128:(cc + 1) * 128],
                             AFP[:, F_R1 + cc * 512 + tcl * 128:
                                 F_R1 + cc * 512 + (tcl + 1) * 128], id32[:],
                             inc=((s_pe, 1) if cc == 7 else ()),
                             label=(f"d_tro_{tt}_{tcl}" if cc == 7 else None))

        def d_scalar(p):
            for tt in range(NTT):
                fo0 = D_FO_S[tt % 2]
                if tt > 1:
                    p.wait(f"d_of_{tt - 2}")
                for ob in range(HOB):
                    g1 = ob * 2
                    p.wait(f"d_mm12_{tt}_{g1}")
                    if ob >= 2:
                        p.wait(f"d_gate_{tt}_{ob - 2}")
                    acc = acc_of(g1)
                    p.op("activation", bf(fo0 + (ob % 2) * 512, 512), acc,
                         AF.Silu, scale=scol["s1"][:, ob:ob + 1],
                         inc=(s_ac, 1), label=f"d_t1_{tt}_{ob}")
                for li in range(2):
                    sc = scol["sh1"] if li == 0 else scol["sh2"]
                    dst0 = D_B if li == 0 else D_A
                    for ob in range(HOB):
                        p.wait(f"d_mmh_{tt}_{li}_{ob}")
                        acc = acc_of(ob % 4)
                        p.op("activation", bf(dst0 + ob * 512, 512), acc,
                             AF.Gelu, scale=sc[:, ob:ob + 1],
                             inc=(s_ac, 1), label=f"d_eph_{tt}_{li}_{ob}")
                    prog.set_label(f"d_geldone_{tt}_{li}", s_ac, s_ac.n)
                for ob in range(8):
                    p.wait(f"d_mmw3_{tt}_{ob}")
                    acc = acc_of(ob % 4)
                    p.op("activation", bf(fo0 + ob * 512, 512), acc, AF.Copy,
                         scale=s3col[:, ob:ob + 1],
                         inc=(s_ac, 1), label=f"d_epw3_{tt}_{ob}")
                for ob in range(8):
                    p.op("activation", bf(D_JK8 + ob * 512, 512),
                         bf(fo0 + ob * 512, 512),
                         AF.Square, inc=(s_ac, 1), label=f"d_fsq_{tt}_{ob}")
                p.wait(f"d_var4_{tt}")
                p.op("activation", R_B[0:1, 0:512], R_B[0:1, 0:512], AF.Sqrt,
                     bias=EPSC[0:1, 0:1], inc=(s_ac, 1), label=f"d_sq4_{tt}")
                p.wait(f"d_nmr4_{tt}")
                if tt > 0:
                    p.wait(f"d_tro_{tt - 1}_3")
                p.dma(fp(F_R1, 4096).rearrange("p (cc t) -> p cc t", cc=8),
                      r1_scr.rearrange("cc p t -> p cc t")[
                          :, :, tt * 512:(tt + 1) * 512],
                      s_ds, label=f"d_r1l_{tt}")
                for tcl in range(4):
                    tcg = tt * 4 + tcl
                    p.wait(f"d_oev_{tt}_{tcl}")
                    p.dma(out[tcg * 128:(tcg + 1) * 128, :],
                          fp(F_OUT[tcg % 2], 1024), s_out,
                          label=f"d_od_{tt}_{tcl}")
            p.wait_sem(s_out, s_out.n)
            p.wait_sem(s_ds, s_ds.n)

        def d_vector(p):
            for tt in range(NTT):
                fo0 = D_FO_S[tt % 2]

                def emit_gate(ob):
                    p.op("tensor_tensor",
                         ABF[:, D_A + ob * 512:D_A + (ob + 1) * 512],
                         bf(fo0 + (ob % 2) * 512, 512),
                         bf(fo0 + 1024 + (ob % 2) * 512, 512), ALU.mult,
                         inc=(s_dv, 1), label=f"d_gate_{tt}_{ob}")
                    prog.set_label(f"d_a_{tt}_{ob}", s_dv, s_dv.n)
                for ob in range(HOB):
                    g2 = ob * 2 + 1
                    p.wait(f"d_t1_{tt}_{ob}")
                    p.wait(f"d_mm12_{tt}_{g2}")
                    if tt > 1 and ob == 0:
                        p.wait(f"d_of_{tt - 2}")
                    acc2 = acc_of(g2)
                    p.op("tensor_scalar", bf(fo0 + 1024 + (ob % 2) * 512, 512),
                         acc2, scol["s2"][:, ob:ob + 1], None, ALU.mult, ALU.bypass,
                         inc=(s_dv, 1), label=f"d_ep2_{tt}_{ob}")
                    if ob > 0:
                        emit_gate(ob - 1)
                emit_gate(HOB - 1)
                p.wait(f"d_str4_{tt}")
                p.op("tensor_scalar_mul", R_A[0:1, 0:512], YPS[0:1, 0:512],
                     1.0 / 1024)
                p.op("tensor_scalar_mul", R_B[0:1, 0:512], YPS[0:1, 512:1024],
                     1.0 / 1024, inc=(s_dv, 1), label=f"d_sre4_{tt}")
                p.op("tensor_tensor", NROWB[0:1, 0:512], R_A[0:1, 0:512],
                     R_A[0:1, 0:512], ALU.mult)
                p.op("memset", NROWB[0:1, 512:1024], 0.0)
                p.op("tensor_tensor", R_B[0:1, 0:512], R_B[0:1, 0:512],
                     NROWB[0:1, 0:512], ALU.subtract)
                p.op("memset", NROWB[0:1, 512:768], 0.0)
                p.op("memset", NROWB[0:1, 768:1024], 0.0, inc=(s_dv, 1),
                     label=f"d_var4_{tt}")
                p.wait(f"d_sq4_{tt}")
                p.op("reciprocal", R_B[0:1, 0:512], R_B[0:1, 0:512])
                p.op("memset", NROWB[0:1, 0:512], 0.0)
                p.op("scalar_tensor_tensor", R_A[0:1, 0:512], R_A[0:1, 0:512],
                     -1.0, R_B[0:1, 0:512], ALU.mult, ALU.mult)
                p.op("memset", NROWB[0:1, 0:256], 0.0)
                p.op("memset", NROWB[0:1, 256:512], 0.0, inc=(s_dv, 1),
                     label=f"d_nmr4_{tt}")
                p.wait(f"d_ab4_{tt}")
                p.op("tensor_copy", fp(F_A4, 512), PSR[:, 0:512],
                     inc=(s_dv, 1), label=f"d_a4e_{tt}")
                p.wait(f"d_bb4_{tt}")
                p.op("tensor_copy", fp(F_B4, 512), PSR[:, 0:512],
                     inc=(s_dv, 1), label=f"d_b4e_{tt}")
                p.wait(f"d_r1l_{tt}")
                for cc in range(8):
                    r1c = AFP[:, F_R1 + cc * 512:F_R1 + (cc + 1) * 512]
                    p.op("tensor_tensor", r1c, r1c, fp(F_B4, 512), ALU.add)
                for cc in range(8):
                    foc = bf(fo0 + cc * 512, 512)
                    scr = fp(F_SCR + (cc % 2) * 512, 512)
                    p.op("tensor_tensor", scr, foc, fp(F_A4, 512), ALU.mult)
                    if cc > 0:
                        pscr = fp(F_SCR + ((cc - 1) % 2) * 512, 512)
                        r1p = AFP[:, F_R1 + (cc - 1) * 512:F_R1 + cc * 512]
                        p.op("tensor_tensor", r1p, r1p, pscr, ALU.add)
                p.op("tensor_tensor", AFP[:, F_R1 + 7 * 512:F_R1 + 8 * 512],
                     AFP[:, F_R1 + 7 * 512:F_R1 + 8 * 512],
                     fp(F_SCR + 512, 512), ALU.add,
                     inc=(s_dv, 1), label=f"d_of_{tt}")
                for tcl in range(4):
                    p.wait(f"d_tro_{tt}_{tcl}")
                    tcg = tt * 4 + tcl
                    if tcg >= 1:
                        p.wait(f"d_od_{(tcg - 1) // 4}_{(tcg - 1) % 4}")
                    p.op("tensor_copy", fp(F_OUT[tcg % 2], 1024), YPS[:, :],
                         inc=(s_dv, 1), label=f"d_oev_{tt}_{tcl}")

        def d_sync(p):
            for idx in range(len(full_seq)):
                if q_of(idx) == 0:
                    d_emit_wd(p, 0, idx)
            p.wait_sem(s_dd[0], s_dd[0].n)
            p.wait_sem(s_dd[1], s_dd[1].n)
            p.wait_sem(s_dxf, s_dxf.n)
            p.wait_sem(s_out, s_out.n)

        blocks.append(("d", {"gpsimd": d_gpsimd, "tensor": d_tensor,
                             "scalar": d_scalar, "vector": d_vector,
                             "sync": d_sync}))

    run_blocks(nc, prog, blocks)
    return nc


_NC_CACHE = None
_LAST_RES = None


def kernel(**inputs):
    global _NC_CACHE
    x = np.ascontiguousarray(np.asarray(inputs["L0_attn_x"], dtype=np.float32))
    B, T, C = x.shape
    t = lambda a: np.ascontiguousarray(np.asarray(a, dtype=np.float32).T)
    r8 = lambda a: np.ascontiguousarray(np.asarray(a, dtype=np.float32).reshape(8, 128))
    r32 = lambda a: np.ascontiguousarray(np.asarray(a, dtype=np.float32).reshape(32, 128))
    shared = {
        "wqT": t(inputs["L0_attn_Wq"]), "wkT": t(inputs["L0_attn_Wk"]),
        "wvT": t(inputs["L0_attn_Wv"]), "wpT": t(inputs["L0_attn_Wp"]),
        "bq": r8(inputs["L0_attn_bq"]), "bk": r8(inputs["L0_attn_bk"]),
        "bv": r8(inputs["L0_attn_bv"]), "bp": r8(inputs["L0_attn_bp"]),
        "w1T": t(inputs["L0_ffn_w1_w"]), "w2T": t(inputs["L0_ffn_w2_w"]),
        "h1T": t(inputs["L0_ffn_h1_w"]), "h2T": t(inputs["L0_ffn_h2_w"]),
        "w3T": t(inputs["L0_ffn_w3_w"]),
        "s1": r32(inputs["L0_ffn_w1_s"]), "s2": r32(inputs["L0_ffn_w2_s"]),
        "sh1": r32(inputs["L0_ffn_h1_s"]), "sh2": r32(inputs["L0_ffn_h2_s"]),
        "s3": r8(inputs["L0_ffn_w3_s"]),
    }
    in_maps = []
    for b in range(4):
        for half in range(2):
            xr = np.ascontiguousarray(
                np.concatenate([x[b, half * 2048:], x[b, :half * 2048]], axis=0))
            m = dict(shared)
            m["x"] = xr
            in_maps.append(m)
    if _NC_CACHE is None:
        _NC_CACHE = build_nc()
    res = run_bass_kernel_spmd(_NC_CACHE, in_maps, core_ids=list(range(8)))
    global _LAST_RES
    _LAST_RES = res
    outp = np.empty((4, T, C), dtype=np.float32)
    for b in range(4):
        for half in range(2):
            outp[b, half * 2048:half * 2048 + 2048] = res.results[2 * b + half]["out"]
    return outp


if __name__ == "__main__":
    nc = build_nc()
    print("built ok, instructions:",
          sum(1 for _ in nc.all_instructions()) if hasattr(nc, "all_instructions") else "?")
